# version 31
# baseline (speedup 1.0000x reference)
import bisect
import numpy as np
import ml_dtypes
from contextlib import ExitStack

import concourse.bass as bass
import concourse.mybir as mybir
from concourse.bass_utils import run_bass_kernel_spmd

F32 = mybir.dt.float32
BF16 = mybir.dt.bfloat16
AF = mybir.ActivationFunctionType
OP = mybir.AluOpType
AX = mybir.AxisListType.X

ENGINES = ("sync", "tensor", "vector", "scalar", "gpsimd")

B_TOT, S_FULL, D, H, HD = 16, 2048, 1024, 16, 64
NCORE = 8
BL = B_TOT // NCORE
F4 = 4 * D



class _Sem:
    __slots__ = ("name", "n", "h", "_last")

    def __init__(self, name):
        self.name = name
        self.n = 0
        self.h = None
        self._last = 0


class _Op:
    __slots__ = ("fn", "waits", "incs", "seq")

    def __init__(self, fn, seq):
        self.fn = fn
        self.waits = []
        self.incs = []
        self.seq = seq


class _Eng:
    def __init__(self, name):
        self.name = name
        self.ops = []
        self.sem = _Sem("S" + name[:2])
        self.inc_seqs = []
        self.inc_vals = []
        self.waited = {}


class Buf:
    __slots__ = ("w", "r", "name", "bank")

    def __init__(self, name="", bank=None):
        self.name = name
        self.w = None
        self.r = {}
        self.bank = bank


class KB:
    def __init__(self):
        self.eng = {n: _Eng(n) for n in ENGINES}
        self.dma_sems = {}
        self.all_sems = [e.sem for e in self.eng.values()]
        self.dr_cols = 0
        self.dr_banks = set()
        self.n_drains = 0
        self.n_mm = 0
        self.hist = []

    def _resolve(self, mk):
        if mk[0] == "s":
            return mk[1], mk[2]
        _, engname, seq = mk
        e = self.eng[engname]
        if e.ops[seq].incs:
            return e.ops[seq].incs[0]
        i = bisect.bisect_left(e.inc_seqs, seq)
        if i < len(e.inc_seqs):
            return e.sem, e.inc_vals[i]
        e.sem.n += 1
        e.ops[seq].incs.append((e.sem, e.sem.n))
        e.inc_seqs.append(seq)
        e.inc_vals.append(e.sem.n)
        return e.sem, e.sem.n

    def _add_wait(self, engname, op, mk):
        if mk[0] == "e" and mk[1] == engname and engname == "tensor":
            return
        sem, val = self._resolve(mk)
        e = self.eng[engname]
        if e.waited.get(sem.name, -1) >= val:
            return
        op.waits.append((sem, val))
        e.waited[sem.name] = val

    def op(self, engname, fn, reads=(), writes=()):
        e = self.eng[engname]
        o = _Op(fn, len(e.ops))
        e.ops.append(o)
        e.sem.n += 1
        o.incs.append((e.sem, e.sem.n))
        e.inc_seqs.append(o.seq)
        e.inc_vals.append(e.sem.n)
        for b in reads:
            if b.w is not None:
                self._add_wait(engname, o, b.w)
        for b in writes:
            if b.w is not None:
                self._add_wait(engname, o, b.w)
            for rm in b.r.values():
                self._add_wait(engname, o, rm)
        m = ("e", engname, o.seq)
        for b in reads:
            b.r[engname] = m
        for b in writes:
            b.w = m
            b.r = {}
        return o

    def dma(self, q, dst_ap, src_ap, reads=(), writes=()):
        pool = self.dma_sems.get(q)
        if pool is None:
            pool = [_Sem(f"D{q[:2]}{j}") for j in range(4)]
            self.dma_sems[q] = pool
            self.all_sems.extend(pool)
            self._dma_idx = getattr(self, "_dma_idx", {})
            self._dma_idx[q] = 0
        idx = self._dma_idx[q] % 4
        self._dma_idx[q] += 1
        dsem = pool[idx]
        e = self.eng[q]
        o = _Op(lambda eng, d=dst_ap, s=src_ap: eng.dma_start(out=d, in_=s),
                len(e.ops))
        e.ops.append(o)
        if dsem.n > 0 and e.waited.get(dsem.name, -1) < dsem.n:
            o.waits.append((dsem, dsem.n))
            e.waited[dsem.name] = dsem.n
        for b in reads:
            if b.w is not None:
                self._add_wait(q, o, b.w)
        for b in writes:
            if b.w is not None:
                self._add_wait(q, o, b.w)
            for rm in b.r.values():
                self._add_wait(q, o, rm)
        dsem.n += 16
        o.incs.append((dsem, dsem.n))
        m = ("s", dsem, dsem.n)
        for b in reads:
            b.r["#" + q] = m
        for b in writes:
            b.w = m
            b.r = {}
        return o

    def _maybe_drain(self, bank, ncols):
        self.hist.append(bank)
        if len(self.hist) > 8:
            self.hist.pop(0)
        if len(self.hist) == 8 and len(set(self.hist)) >= 4:
            self.dr_cols = 0
            self.dr_banks = set()
            return
        banks = self.dr_banks | {bank}
        thresh = 1536 * min(len(banks), 4)
        if self.dr_cols + ncols > thresh:
            e = self.eng["tensor"]
            e.ops.append(_Op(lambda eng: eng.drain(), len(e.ops)))
            self.n_drains += 1
            self.dr_cols = 0
            self.dr_banks = set()
            self.hist = []

    def mm(self, out_ap, lhsT_ap, rhs_ap, ncols, psum_buf, start, stop,
           reads=(), tile_position=None):
        self._maybe_drain(psum_buf.bank, ncols)
        self.dr_cols += ncols
        self.dr_banks.add(psum_buf.bank)
        self.n_mm += 1
        if tile_position is None:
            fn = (lambda eng, o=out_ap, l=lhsT_ap, r=rhs_ap, st=start,
                  sp=stop: eng.matmul(o, l, r, start=st, stop=sp))
        else:
            fn = (lambda eng, o=out_ap, l=lhsT_ap, r=rhs_ap, st=start,
                  sp=stop, tp=tile_position:
                  eng.matmul(o, l, r, start=st, stop=sp, tile_position=tp))
        return self.op("tensor", fn, reads=reads, writes=(psum_buf,))

    def barrier(self, dma=True):
        marks = []
        for e in self.eng.values():
            if e.ops:
                marks.append(("e", e.name, len(e.ops) - 1))
        if dma:
            for pool in self.dma_sems.values():
                for dsem in pool:
                    if dsem.n:
                        marks.append(("s", dsem, dsem.n))
        resolved = [self._resolve(mk) for mk in marks]
        for e in self.eng.values():
            o = _Op(lambda eng: eng.nop(), len(e.ops))
            e.ops.append(o)
            for sem, val in resolved:
                if e.waited.get(sem.name, -1) < val:
                    o.waits.append((sem, val))
                    e.waited[sem.name] = val
        self.dr_cols = 0
        self.dr_banks = set()

    def replay(self, block):
        kb = self

        def mk_body(engname):
            def body(eng):
                for o in kb.eng[engname].ops:
                    for sem, val in o.waits:
                        eng.wait_ge(sem.h, val)
                    inst = o.fn(eng)
                    for sem, val in o.incs:
                        inst.then_inc(sem.h, val - sem._last)
                        sem._last = val
                for q, pool in kb.dma_sems.items():
                    if q == engname:
                        for dsem in pool:
                            if dsem.n > 0:
                                eng.wait_ge(dsem.h, dsem.n)
            return body

        for engname in ENGINES:
            getattr(block, engname)(mk_body(engname))



def build_nc(S=S_FULL, detect_races=False, debug_outputs=False):
    assert S % 512 == 0
    nst = S // 256
    nqs = S // 512
    QCOLS = max(8 * S, 16384)

    nc = bass.Bass("TRN2", target_bir_lowering=False,
                   detect_race_conditions=detect_races)

    d_xt = nc.dram_tensor("xt", [BL, D, S], F32, kind="ExternalInput")
    d_wfc = nc.dram_tensor("wfc", [D, F4], F32, kind="ExternalInput")
    d_wpr = nc.dram_tensor("wpr", [F4, D], F32, kind="ExternalInput")
    d_wq = nc.dram_tensor("wq", [D, D], F32, kind="ExternalInput")
    d_wk = nc.dram_tensor("wk", [D, D], F32, kind="ExternalInput")
    d_wv = nc.dram_tensor("wv", [D, D], F32, kind="ExternalInput")
    d_wo = nc.dram_tensor("wo", [D, D], F32, kind="ExternalInput")
    d_bq = nc.dram_tensor("bq", [128, 8], F32, kind="ExternalInput")
    d_bk = nc.dram_tensor("bk", [128, 8], F32, kind="ExternalInput")
    d_bo = nc.dram_tensor("bo", [128, 8], F32, kind="ExternalInput")
    d_bv = nc.dram_tensor("bv", [1, D], F32, kind="ExternalInput")
    d_mask = nc.dram_tensor("mask", [128, 128], BF16, kind="ExternalInput")
    d_ones = nc.dram_tensor("ones", [128, 1], BF16, kind="ExternalInput")
    d_onesr = nc.dram_tensor("onesr", [1, 128], BF16, kind="ExternalInput")
    d_onesf = nc.dram_tensor("onesf", [1, 128], F32, kind="ExternalInput")
    d_onesc = nc.dram_tensor("onesc", [128, 1], F32, kind="ExternalInput")
    d_cf32 = nc.dram_tensor("cf32", [128, 6], F32, kind="ExternalInput")
    d_out = nc.dram_tensor("out", [BL, D, S], F32, kind="ExternalOutput")
    scratch_kind = dict(kind="ExternalOutput") if debug_outputs else {}
    if debug_outputs:
        d_dbgq = nc.dram_tensor("dbgq", [128, 8 * S], BF16, kind="ExternalOutput")
        d_dbgk = nc.dram_tensor("dbgk", [128, 8 * S], BF16, kind="ExternalOutput")
        d_dbgv = nc.dram_tensor("dbgv", [128, (S // 128) * 1040], BF16,
                                kind="ExternalOutput")
        d_dbgw = nc.dram_tensor("dbgw", [128, 8 * 512], BF16,
                                kind="ExternalOutput")
        d_dbgc = nc.dram_tensor("dbgc", [128, BL], F32, kind="ExternalOutput")
    d_x1t = nc.dram_tensor("x1t", [BL, D, S], F32, **scratch_kind)
    d_x2t = nc.dram_tensor("x2t", [BL, D, S], F32, **scratch_kind)
    d_wfcb = nc.dram_tensor("wfcb", [D, F4], BF16)
    d_wprb = nc.dram_tensor("wprb", [F4, D], BF16)
    d_cfl = nc.dram_tensor("cfl", [1, 256], F32)

    kb = KB()
    top = ExitStack()
    bufs = {}

    def mksbuf(stack, name, shape, dt):
        t = stack.enter_context(nc.sbuf_tensor(name, shape, dt))
        bufs[name] = Buf(name)
        return t

    def B(n):
        return bufs[n]

    with top:
        c_bq = mksbuf(top, "c_bq", [128, 8], F32)
        c_bk = mksbuf(top, "c_bk", [128, 8], F32)
        c_bo = mksbuf(top, "c_bo", [128, 8], F32)
        c_mask = mksbuf(top, "c_mask", [128, 128], BF16)
        c_ones = mksbuf(top, "c_ones", [128, 1], BF16)
        c_onesr = mksbuf(top, "c_onesr", [1, 128], BF16)
        c_onesf = mksbuf(top, "c_onesf", [1, 128], F32)
        c_onesc = mksbuf(top, "c_onesc", [128, 1], F32)
        c_f32 = mksbuf(top, "c_f32", [128, 6], F32)
        s_negc = mksbuf(top, "s_negc", [128, BL], F32)
        s_cmax = mksbuf(top, "s_cmax", [128, 2], F32)
        s_crow = mksbuf(top, "s_crow", [1, 256], F32)
        s_cden = mksbuf(top, "s_cden", [1, 8], F32)
        s_cbf = mksbuf(top, "s_cbf", [1, 4], BF16)
        bufs["d_cfl"] = Buf("d_cfl")
        PW = []
        for j in range(4):
            t = top.enter_context(nc.psum_tensor(f"pw{j}", [128, 1024], F32))
            PW.append(t)
        PB = []
        for i in range(8):
            bufs[f"pb{i}"] = Buf(f"pb{i}", bank=i)
            PB.append(PW[i // 2][:, (i % 2) * 512:(i % 2) * 512 + 512])

        for nm, t, dd in (("c_bq", c_bq, d_bq), ("c_bk", c_bk, d_bk),
                          ("c_bo", c_bo, d_bo), ("c_mask", c_mask, d_mask),
                          ("c_ones", c_ones, d_ones),
                          ("c_onesr", c_onesr, d_onesr),
                          ("c_onesf", c_onesf, d_onesf),
                          ("c_onesc", c_onesc, d_onesc),
                          ("c_f32", c_f32, d_cf32)):
            kb.dma("gpsimd", t[:], dd[:], writes=(B(nm),))

        def kb_mms(lst):
            for o, l, rr, nc_, pb, st_, sp, rd in lst:
                kb.mm(o, l, rr, nc_, pb, start=st_, stop=sp, reads=rd)

        def emit_ln(xt_t, xn, lnx_t, lnxn, xsq_t, xsqn,
                    ab_t, abn, rows_t, rowsn, tmp_t, tmpn):
            a1m, a2p, a2m, bpre, bmms, bpost, c = emit_ln_parts(
                xt_t, xn, lnx_t, lnxn, xsq_t, xsqn, ab_t, abn,
                rows_t, rowsn, tmp_t, tmpn)
            kb_mms(a1m()); a2p("vector"); kb_mms(a2m()); bpre()
            kb_mms(bmms()); bpost(); c()

        def emit_ln_parts(xt_t, xn, lnx_t, lnxn, xsq_t, xsqn,
                          ab_t, abn, rows_t, rowsn, tmp_t, tmpn):
            xB, lnxB = B(xn), B(lnxn)
            xsqB, abB, rowsB, tmpB = B(xsqn), B(abn), B(rowsn), B(tmpn)
            p6B, p7B = B("pb6"), B("pb7")

            def a1_mms():
              return [(PB[6][0:1, 0:256], c_onesc[:],
                       xt_t[:, dk * 256:(dk + 1) * 256], 256, p6B,
                       dk == 0, dk == 7, (xB, B("c_onesc")))
                      for dk in range(8)]

            def a2_pre(eng="scalar"):
              if eng == "scalar":
                  kb.op("scalar", lambda e, a=xsq_t, b_=xt_t: e.activation(
                    a[:, 0:2048], b_[:, 0:2048], AF.Square),
                      reads=(xB,), writes=(xsqB,))
              else:
                  kb.op(eng, lambda e, a=xsq_t, b_=xt_t:
                        e.tensor_tensor(
                    a[:, 0:2048], b_[:, 0:2048], b_[:, 0:2048], OP.mult),
                      reads=(xB,), writes=(xsqB,))

            def a2_mms():
              return [(PB[7][0:1, 0:256], c_ones[:, 0:1],
                       xsq_t[:, dk * 256:(dk + 1) * 256], 256, p7B,
                       dk == 0, dk == 7, (xsqB, B("c_ones")))
                      for dk in range(8)]
            def b_pre():
              r = rows_t
              kb.op("vector", lambda e: e.tensor_scalar(
                  r[0:1, 0:256], PB[6][0:1, 0:256], 1.0 / 1024, None, OP.mult),
                  reads=(p6B,), writes=(rowsB,))
              kb.op("vector", lambda e: e.tensor_scalar(
                  r[0:1, 256:512], PB[7][0:1, 0:256], 1.0 / 1024, None,
                  OP.mult), reads=(p7B,), writes=(rowsB,))
              kb.op("vector", lambda e: e.tensor_tensor(
                  r[0:1, 512:768], r[0:1, 0:256], r[0:1, 0:256], OP.mult),
                  reads=(rowsB,), writes=(rowsB,))
              kb.op("vector", lambda e: e.tensor_tensor(
                  r[0:1, 256:512], r[0:1, 256:512], r[0:1, 512:768],
                  OP.subtract), reads=(rowsB,), writes=(rowsB,))
              kb.op("vector", lambda e: e.tensor_scalar(
                  r[0:1, 256:512], r[0:1, 256:512], 1e-5, None, OP.add),
                  reads=(rowsB,), writes=(rowsB,))
              kb.op("scalar", lambda e, r=r: e.activation(
                  r[0:1, 256:512], r[0:1, 256:512], AF.Ln),
                  reads=(rowsB,), writes=(rowsB,))
              kb.op("scalar", lambda e, r=r: e.activation(
                  r[0:1, 256:512], r[0:1, 256:512], AF.Exp,
                  scale=c_f32[0:1, 4:5]),
                  reads=(rowsB, B("c_f32")), writes=(rowsB,))
              kb.op("vector", lambda e, r=r: e.tensor_tensor(
                  r[0:1, 0:256], r[0:1, 0:256], r[0:1, 256:512], OP.mult),
                  reads=(rowsB,), writes=(rowsB,))

            def b_mms():
              r = rows_t
              return [
                  (PB[6][0:128, 0:256], c_onesf[:], r[0:1, 256:512], 256,
                   p6B, True, True, (rowsB, B("c_onesf"))),
                  (PB[7][0:128, 0:256], c_onesf[:], r[0:1, 0:256], 256,
                   p7B, True, True, (rowsB, B("c_onesf")))]

            def b_post():
              kb.op("vector", lambda e, a=ab_t: e.tensor_copy(
                  a[:, 0:512].rearrange("p (c n) -> p c n", c=2),
                  PW[3].rearrange("p (c n) -> p c n", c=4)[:, 0::2, 0:256]),
                  reads=(p6B, p7B), writes=(abB,))

            def stage_c(d0=0, d1=8):
              for dk in range(d0, d1):
                  sl = slice(dk * 256, (dk + 1) * 256)
                  lnxdB = bufs.setdefault(lnxn + f"d{dk}",
                                          Buf(lnxn + f"d{dk}"))
                  kb.op("vector", lambda e, sl=sl: e.tensor_tensor(
                      tmp_t[:, sl], xt_t[:, sl], ab_t[:, 0:256], OP.mult),
                      reads=(xB, abB), writes=(tmpB,))
                  kb.op("vector", lambda e, sl=sl: e.tensor_tensor(
                      lnx_t[:, sl], tmp_t[:, sl], ab_t[:, 256:512],
                      OP.subtract), reads=(tmpB, abB),
                      writes=(lnxB, lnxdB))
            return a1_mms, a2_pre, a2_mms, b_pre, b_mms, \
                b_post, stage_c

        def emit_ffn(src_dram, dst_dram, first, tag):
            ph = ExitStack()
            wfc = mksbuf(ph, tag + "wfc", [128, 8 * F4], BF16)
            wpr = mksbuf(ph, tag + "wpr", [128, 32 * D], BF16)
            for j in range(8):
                bufs[tag + f"wfg{j}"] = Buf(tag + f"wfg{j}")
                bufs[tag + f"wpr{j}"] = Buf(tag + f"wpr{j}")
                bufs[tag + f"wfcc{j}"] = Buf(tag + f"wfcc{j}")
                bufs[tag + f"wprc{j}"] = Buf(tag + f"wprc{j}")
            if first:
              with ExitStack() as st2:
                stg = mksbuf(st2, tag + "wst", [128, 2 * F4], F32)
                bufs[tag + "wst0"] = Buf(tag + "wst0")
                bufs[tag + "wst1"] = Buf(tag + "wst1")
                wbacks = []
                for dk in range(8):
                    sl = slice((dk % 2) * F4, (dk % 2 + 1) * F4)
                    stB = B(tag + f"wst{dk % 2}")
                    kb.dma(("gpsimd", "scalar", "sync")[dk % 3],
                           stg[:, sl],
                           d_wfc.ap().rearrange("(c p) n -> p c n",
                                                p=128)[:, dk, :],
                           writes=(stB,))
                    eng = "vector" if dk % 2 == 0 else "scalar"
                    kb.op(eng, lambda e, a=wfc[:, dk * F4:(dk + 1) * F4],
                          s=stg[:, sl], eng=eng:
                          (e.tensor_copy(a, s) if eng == "vector"
                           else e.copy(a, s)),
                          reads=(stB,), writes=(B(tag + f"wfcc{dk}"),))
                    wbacks.append((d_wfcb.ap().rearrange(
                        "(c p) n -> p c n", p=128)[:, dk, :],
                        wfc[:, dk * F4:(dk + 1) * F4],
                        B(tag + f"wfcc{dk}")))
                for g in range(8):
                    sl = slice((g % 2) * F4, (g % 2 + 1) * F4)
                    stB = B(tag + f"wst{g % 2}")
                    kb.dma(("scalar", "sync", "gpsimd")[g % 3],
                           stg[:, sl].rearrange("p (c n) -> p c n", c=4),
                           d_wpr.ap().rearrange(
                               "(c p) n -> p c n",
                               p=128)[:, 4 * g:4 * g + 4, :],
                           writes=(stB,))
                    eng = "vector" if g % 2 == 0 else "scalar"
                    kb.op(eng, lambda e, a=wpr[:, g * F4:(g + 1) * F4],
                          s=stg[:, sl], eng=eng:
                          (e.tensor_copy(a, s) if eng == "vector"
                           else e.copy(a, s)),
                          reads=(stB,), writes=(B(tag + f"wprc{g}"),))
                    wbacks.append((d_wprb.ap().rearrange(
                        "(c p) n -> p c n",
                        p=128)[:, 4 * g:4 * g + 4, :],
                        wpr[:, g * F4:(g + 1) * F4].rearrange(
                            "p (c n) -> p c n", c=4),
                        B(tag + f"wprc{g}")))
                for dst, src, rb in wbacks:
                    kb.dma("gpsimd", dst, src, reads=(rb,))
                kb.barrier(dma=False)
                del bufs[tag + "wst"]
                del bufs[tag + "wst0"]
                del bufs[tag + "wst1"]
            else:
                for j in range(8):
                    kb.dma("gpsimd",
                           wfc.ap().rearrange("p (c n) -> p c n",
                                              c=8)[:, :, j * 512:
                                                   (j + 1) * 512],
                           d_wfcb.ap().rearrange(
                               "(c p) n -> p c n",
                               p=128)[:, :, j * 512:(j + 1) * 512],
                           writes=(B(tag + f"wfg{j}"),))
                    kb.dma("scalar",
                           wpr[:, j * F4:(j + 1) * F4].rearrange(
                               "p (c n) -> p c n", c=4),
                           d_wprb.ap().rearrange(
                               "(c p) n -> p c n",
                               p=128)[:, 4 * j:4 * j + 4, :],
                           writes=(B(tag + f"wpr{j}"),))

            x_in = [mksbuf(ph, f"{tag}x{i}", [128, 2048], F32)
                    for i in range(3)]
            xbf = mksbuf(ph, tag + "xbf", [128, 2048], BF16)
            xsq = mksbuf(ph, tag + "xsq", [128, 2048], BF16)
            lnx = mksbuf(ph, tag + "lnx", [128, 2 * 2048], BF16)
            bufs[tag + "lnx0"] = Buf(tag + "lnx0")
            bufs[tag + "lnx1"] = Buf(tag + "lnx1")
            del bufs[tag + "lnx"]
            h_sb = mksbuf(ph, tag + "h", [128, 2 * 1024], BF16)
            bufs[tag + "h0"] = Buf(tag + "h0")
            bufs[tag + "h1"] = Buf(tag + "h1")
            del bufs[tag + "h"]
            yb = mksbuf(ph, tag + "y", [128, 2048], F32)
            mb = mksbuf(ph, tag + "m", [128, 2048], F32)
            ab = mksbuf(ph, tag + "ab", [128, 512], F32)
            rows = mksbuf(ph, tag + "rw", [1, 768], F32)

            tiles = [(b, st) for b in range(BL) for st in range(nst)]
            nt = len(tiles)

            def load_tile(i):
                b, st = tiles[i]
                par = i % 3
                kb.dma("sync",
                       x_in[par].ap().rearrange("p (c n) -> p c n", c=8),
                       src_dram[b, :, st * 256:(st + 1) * 256].rearrange(
                           "(c p) n -> p c n", p=128),
                       writes=(B(f"{tag}x{par}"),))

            def ln_stages(i):
                par = i % 3
                lp = i % 2
                return emit_ln_parts(
                    x_in[par], f"{tag}x{par}",
                    lnx[:, lp * 2048:(lp + 1) * 2048], tag + f"lnx{lp}",
                    xsq, tag + "xsq", ab, tag + "ab",
                    rows, tag + "rw", mb, tag + "m")

            for j in range(min(3, nt)):
                load_tile(j)
            a1m, a2p, a2m, bpre_f, bmms_f, bpost_f, c_f = ln_stages(0)
            kb_mms(a1m()); a2p("vector"); kb_mms(a2m()); bpre_f()
            kb_mms(bmms_f()); bpost_f(); c_f()
            pending_spike = [None]


            for i in range(nt):
                b, st = tiles[i]
                par = i % 3
                xb, xt = B(f"{tag}x{par}"), x_in[par]
                hB = (B(tag + "h0"), B(tag + "h1"))
                yB, mB = B(tag + "y"), B(tag + "m")
                zB, ZB = B(tag + "xbf"), B(tag + "xsq")

                def h_mms(r):
                    out = []
                    lp = i % 2
                    lnxv = lnx[:, lp * 2048:(lp + 1) * 2048]
                    for pair in range(2):
                        for dk in range(8):
                            lnxdB = bufs.setdefault(
                                tag + f"lnx{lp}d{dk}",
                                Buf(tag + f"lnx{lp}d{dk}"))
                            for fr in (2 * pair, 2 * pair + 1):
                                fk = r * 4 + fr
                                pt, pc = PB[pair], (fr % 2) * 256
                                out.append((pt[:, pc:pc + 256],
                                            wfc[:, dk * F4 + fk * 128:
                                                dk * F4 + (fk + 1) * 128],
                                            lnxv[:, dk * 256:
                                                 (dk + 1) * 256],
                                            256, B(f"pb{pair}"),
                                            dk == 0 and fr % 2 == 0,
                                            dk == 7 and fr % 2 == 1,
                                            (B(tag + f"wfg{r}"), lnxdB)))
                    return out

                def y_mms(r):
                    out = []
                    slot = r % 2
                    for fr in range(4):
                        fk = r * 4 + fr
                        for dk in range(8):
                            pt = PB[2 + dk // 2]
                            pc = (dk % 2) * 256
                            out.append((pt[:, pc:pc + 256],
                                        wpr[:, fk * D + dk * 128:
                                            fk * D + (dk + 1) * 128],
                                        h_sb[:, slot * 1024 + fr * 256:
                                             slot * 1024 + (fr + 1) * 256],
                                        256, B(f"pb{2 + dk // 2}"),
                                        r == 0 and fr == 0 and dk % 2 == 0,
                                        r == 7 and fr == 3 and dk % 2 == 1,
                                        (B(tag + f"wpr{r}"), hB[fr // 2])))
                    return out

                def emit_mms(a, bl, extra=(), pre=5, front=False):
                    merged = []
                    ia = ib = 0
                    for j in range(min(pre, len(a))):
                        merged.append(a[ia]); ia += 1
                    while ia < len(a) or ib < len(bl):
                        if ib < len(bl):
                            merged.append(bl[ib]); ib += 1
                        if ia < len(a):
                            merged.append(a[ia]); ia += 1
                    if extra:
                        span = len(merged) // 2 if front else len(merged)
                        step = max(1, span // len(extra))
                        woven, ei = [], 0
                        for j, m in enumerate(merged):
                            woven.append(m)
                            if ei < len(extra) and j % step == step - 1:
                                woven.append(extra[ei]); ei += 1
                        woven.extend(extra[ei:])
                        merged = woven
                    kb_mms(merged)

                def relu_round(r):
                    slot = r % 2
                    for pair in range(2):
                        dst = h_sb[:, slot * 1024 + pair * 512:
                                   slot * 1024 + (pair + 1) * 512]
                        kb.op("scalar", lambda e, a=dst, pair=pair:
                              e.activation(
                            a, PW[0][:, pair * 512:(pair + 1) * 512],
                            AF.Relu),
                            reads=(B(f"pb{pair}"),), writes=(hB[pair],))
                        kb.op("scalar", lambda e, a=dst: e.activation(
                            a, a, AF.Square),
                            reads=(hB[pair],), writes=(hB[pair],))

                stages = ln_stages(i + 1) if i + 1 < nt else None
                emit_mms(h_mms(0), [])
                relu_round(0)
                for r in range(1, 8):
                    extra = ()
                    if stages is not None:
                        if r == 1:
                            extra = stages[0]()
                        elif r == 6:
                            extra = stages[2]()
                        elif r == 7:
                            extra = stages[4]()
                    emit_mms(h_mms(r), y_mms(r - 1), extra,
                             front=(r in (6, 7)))
                    relu_round(r)
                    if pending_spike[0] and r <= 5:
                        k_n = max(1, -(-len(pending_spike[0]) // (6 - r)))
                        for t_ in pending_spike[0][:k_n]:
                            t_()
                        del pending_spike[0][:k_n]
                        if not pending_spike[0]:
                            pending_spike[0] = None
                    if stages is not None:
                        if r == 5:
                            stages[1]("vector")
                        elif r == 6:
                            stages[3]()
                        elif r == 7:
                            stages[5]()
                            stages[6]()
                emit_mms([], y_mms(7))
                if pending_spike[0]:
                    for t_ in pending_spike[0]:
                        t_()
                    pending_spike[0] = None
                for half2 in range(2):
                    kb.op("vector", lambda e, half2=half2, yb=yb:
                          e.tensor_scalar(
                        yb[:, half2 * 1024:(half2 + 1) * 1024],
                        PW[1 + half2][:, 0:1024], 2.0, -1.2,
                        OP.mult, OP.add),
                        reads=(B(f"pb{2 + 2 * half2}"),
                               B(f"pb{3 + 2 * half2}")),
                        writes=(yB,))

                def spike_thunks(i=i, b=b, st=st, xb=xb, xt=xt):
                    th = []
                    th.append(lambda: kb.op("vector", lambda e, mb=mb,
                        yb=yb: e.tensor_scalar(
                        mb[:], yb[:], -0.8, None, OP.add),
                        reads=(yB,), writes=(mB,)))
                    th.append(lambda: kb.op("scalar", lambda e, xbf=xbf,
                        mb=mb: e.activation(xbf[:], mb[:], AF.Sign),
                        reads=(mB,), writes=(zB,)))
                    th.append(lambda: kb.op("vector", lambda e, xsq=xsq,
                        xbf=xbf: e.tensor_copy(xsq[:], xbf[:]),
                        reads=(zB,), writes=(ZB,)))
                    for _t in range(4):
                        th.append(lambda: kb.op("vector", lambda e, mb=mb,
                            xbf=xbf: e.scalar_tensor_tensor(
                            mb[:], mb[:], 0.9, xbf[:], OP.mult,
                            OP.subtract),
                            reads=(mB, zB), writes=(mB,)))
                        th.append(lambda: kb.op("vector", lambda e, mb=mb,
                            yb=yb: e.tensor_tensor(
                            mb[:], mb[:], yb[:], OP.add),
                            reads=(mB, yB), writes=(mB,)))
                        th.append(lambda: kb.op("scalar", lambda e, xbf=xbf,
                            mb=mb: e.activation(xbf[:], mb[:], AF.Sign),
                            reads=(mB,), writes=(zB,)))
                        th.append(lambda: kb.op("vector", lambda e, xsq=xsq,
                            xbf=xbf: e.tensor_tensor(
                            xsq[:], xsq[:], xbf[:], OP.add),
                            reads=(ZB, zB), writes=(ZB,)))
                    th.append(lambda: kb.op("vector", lambda e, mb=mb,
                        xsq=xsq: e.tensor_scalar(
                        mb[:], xsq[:], 0.05, 0.25, OP.mult, OP.add),
                        reads=(ZB,), writes=(mB,)))
                    th.append(lambda: kb.op("vector", lambda e, xt=xt,
                        mb=mb: e.tensor_tensor(
                        xt[:], xt[:], mb[:], OP.add),
                        reads=(xb, mB), writes=(xb,)))
                    th.append(lambda: kb.dma(
                        "sync", dst_dram[b, :, st * 256:(st + 1) * 256]
                        .rearrange("(c p) n -> p c n", p=128),
                        xt.ap().rearrange("p (c n) -> p c n", c=8),
                        reads=(xb,)))
                    if i + 3 < nt:
                        th.append(lambda: load_tile(i + 3))
                    return th

                if i + 1 < nt:
                    pending_spike[0] = spike_thunks()
                else:
                    for t_ in spike_thunks():
                        t_()
            kb.barrier()
            ph.close()
            for n in ("wfc", "wpr", "x0", "x1", "x2", "xbf", "xsq",
                      "lnx0", "lnx1", "h0", "h1", "y", "m", "ab", "rw"):
                del bufs[tag + n]
            for n in [k2 for k2 in bufs if k2.startswith(tag + "lnx")]:
                del bufs[n]
            for j in range(8):
                del bufs[tag + f"wfg{j}"]
                del bufs[tag + f"wpr{j}"]
                del bufs[tag + f"wfcc{j}"]
                del bufs[tag + f"wprc{j}"]

        def emit_attn():
            ph = ExitStack()
            q_buf = mksbuf(ph, "aq", [128, QCOLS], BF16)
            k_buf = mksbuf(ph, "ak", [128, 8 * S], BF16)
            v_buf = mksbuf(ph, "av", [128, (S // 128) * 1040], BF16)
            x_in = [mksbuf(ph, f"ax{i}", [128, 2048], F32) for i in range(2)]
            lnx = mksbuf(ph, "alnx", [128, 2 * 2048], BF16)
            bufs["alnx0"] = Buf("alnx0")
            bufs["alnx1"] = Buf("alnx1")
            del bufs["alnx"]
            xbf = mksbuf(ph, "axbf", [128, 2048], BF16)
            xsq = mksbuf(ph, "axsq", [128, 2048], BF16)
            ab = mksbuf(ph, "aab", [128, 512], F32)
            rows = mksbuf(ph, "arw", [1, 768], F32)
            atmp = mksbuf(ph, "atmp", [128, 2048], F32)
            astg = mksbuf(ph, "astg", [128, 2048], F32)
            bufs["astg0"] = Buf("astg0")
            bufs["astg1"] = Buf("astg1")
            c_bvb = mksbuf(ph, "abvb", [1, D], BF16)

            kb.dma("gpsimd", atmp[0:1, 0:1024], d_bv[:], writes=(B("atmp"),))
            kb.op("vector", lambda e: e.tensor_copy(c_bvb[:],
                                                    atmp[0:1, 0:1024]),
                  reads=(B("atmp"),), writes=(B("abvb"),))

            for b in range(BL):
                qkvstack = ExitStack()
                wqb = mksbuf(qkvstack, f"awq{b}", [128, 8192], BF16)
                wkb = mksbuf(qkvstack, f"awk{b}", [128, 8192], BF16)
                wvb = mksbuf(qkvstack, f"awv{b}", [128, 8192], BF16)
                for nm in (f"awq{b}", f"awk{b}", f"awv{b}"):
                    for ch in range(8):
                        bufs[f"{nm}c{ch}"] = Buf(f"{nm}c{ch}")
                stjobs = ([j for ch in range(8)
                           for j in ((f"awq{b}", d_wq, wqb, ch),
                                     (f"awk{b}", d_wk, wkb, ch))]
                          + [(f"awv{b}", d_wv, wvb, ch) for ch in range(8)])
                for st0_ in range(min(2, nst)):
                    kb.dma("sync",
                           x_in[st0_ % 2].ap().rearrange(
                               "p (c n) -> p c n", c=8),
                           d_x1t[b, :, st0_ * 256:
                                 (st0_ + 1) * 256].rearrange(
                               "(c p) n -> p c n", p=128),
                           writes=(B(f"ax{st0_ % 2}"),))
                for idx, (nm, ddr, wt, ch) in enumerate(stjobs):
                    pp = idx % 2
                    sg = astg[:, pp * 1024:(pp + 1) * 1024]
                    sgB = B(f"astg{pp}")
                    kb.dma("gpsimd" if pp == 0 else "scalar",
                           sg, ddr.ap().rearrange("(c p) n -> p c n",
                                                  p=128)[:, ch, :],
                           writes=(sgB,))
                    kb.op("vector", lambda e, w=wt, ch=ch, sg=sg:
                          e.tensor_copy(
                              w[:, ch * 1024:(ch + 1) * 1024], sg),
                          reads=(sgB,), writes=(B(f"{nm}c{ch}"),))

                def a_load(st):
                    par = st % 2
                    kb.dma("sync",
                           x_in[par].ap().rearrange("p (c n) -> p c n", c=8),
                           d_x1t[b, :, st * 256:(st + 1) * 256].rearrange(
                               "(c p) n -> p c n", p=128),
                           writes=(B(f"ax{par}"),))

                def a_ln_parts(st):
                    par = st % 2
                    lp = st % 2
                    return emit_ln_parts(
                        x_in[par], f"ax{par}",
                        lnx[:, lp * 2048:(lp + 1) * 2048], f"alnx{lp}",
                        xsq, "axsq", ab, "aab",
                        rows, "arw", atmp, "atmp")

                def a_ln(st):
                    a1m, a2p, a2m, bpre, bmms, bpost, c = a_ln_parts(st)
                    kb_mms(a1m()); a2p(); kb_mms(a2m()); bpre()
                    kb_mms(bmms()); bpost(); c()

                a_ln(0)
                for st in range(nst):
                    stg = a_ln_parts(st + 1) if st + 1 < nst else None
                    lnxv = lnx[:, (st % 2) * 2048:(st % 2) * 2048 + 2048]
                    lnxB_st = B(f"alnx{st % 2}")
                    for nm, wt, bias, bias_n, dstb, dst in (
                            (f"awq{b}", wqb, c_bq, "c_bq", "aq", q_buf),
                            (f"awk{b}", wkb, c_bk, "c_bk", "ak", k_buf)):
                        for grp in range(2):
                            for dk in range(8):
                                for hj in range(4):
                                    hc = grp * 4 + hj
                                    kb.mm(PB[hj][:, 0:256],
                                          wt[:, dk * 1024 + hc * 128:
                                             dk * 1024 + (hc + 1) * 128],
                                          lnxv[:, dk * 256:(dk + 1) * 256],
                                          256, B(f"pb{hj}"),
                                          start=(dk == 0), stop=(dk == 7),
                                          reads=(B(f"{nm}c{dk}"),
                                                 bufs.setdefault(
                                                     f"alnx{st % 2}d{dk}",
                                                     Buf(f"alnx{st % 2}"
                                                         f"d{dk}"))))
                            for hj in range(4):
                                hc = grp * 4 + hj
                                kb.op("scalar", lambda e, hc=hc, hj=hj,
                                      w=dst, bs=bias, st=st: e.activation(
                                          w[:, hc * S + st * 256:
                                            hc * S + (st + 1) * 256],
                                          PB[hj][:, 0:256], AF.Identity,
                                          bias=bs[:, hc:hc + 1]),
                                      reads=(B(f"pb{hj}"), B(bias_n)),
                                      writes=(B(dstb),))
                    if stg is not None:
                        kb_mms(stg[0]())
                        stg[1]("vector")
                        kb_mms(stg[2]())
                        stg[3]()
                    for scl in range(2):
                        sc = st * 2 + scl
                        for dk in range(8):
                            if scl == 1 and dk == 2 and stg is not None:
                                kb_mms(stg[4]())
                            for hf in range(2):
                                kb.mm(PB[4 + hf][:, 0:512],
                                      lnxv[:, dk * 256 + scl * 128:
                                          dk * 256 + scl * 128 + 128],
                                      wvb[:, dk * 1024 + hf * 512:
                                          dk * 1024 + (hf + 1) * 512],
                                      512, B(f"pb{4 + hf}"),
                                      start=(dk == 0), stop=False,
                                      reads=(B(f"awv{b}c{dk}"),
                                             bufs.setdefault(
                                                 f"alnx{st % 2}d{dk}",
                                                 Buf(f"alnx{st % 2}"
                                                     f"d{dk}"))))
                        for hf in range(2):
                            kb.mm(PB[4 + hf][:, 0:512], c_onesr[:],
                                  c_bvb[:, hf * 512:(hf + 1) * 512], 512,
                                  B(f"pb{4 + hf}"), start=False, stop=True,
                                  reads=(B("abvb"), B("c_onesr")))
                        kb.op("scalar", lambda e, sc=sc: e.copy(
                            v_buf.ap().rearrange(
                                "p (a c) -> p a c", c=65)[
                                :, sc * 16:(sc + 1) * 16, 0:64],
                            PW[2][:, 0:1024].rearrange(
                                "p (a c) -> p a c", c=64)),
                            reads=(B("pb4"), B("pb5")),
                            writes=(B("av"),))
                    if stg is not None:
                        stg[5]()
                        stg[6]()
                    if st + 2 < nst:
                        a_load(st + 2)
                kb.op("vector", lambda e: e.memset(
                    v_buf.ap().rearrange("p (a c) -> p a c",
                                         c=65)[:, :, 64:65], 1.0),
                    reads=(), writes=(B("av"),))

                bufs["axbf0"] = Buf("axbf0")
                bufs["axbf1"] = Buf("axbf1")
                cjobs = [(j, srcb, src, hc, qq)
                         for j, (srcb, src) in enumerate(
                             (("aq", q_buf), ("ak", k_buf)))
                         for hc in range(8) for qq in range(S // 512)]

                def c_sq(i):
                    j, srcb, src, hc, qq = cjobs[i]
                    sl = slice(hc * S + qq * 512, hc * S + qq * 512 + 512)
                    pp = i % 2
                    xsl = xbf[:, pp * 512:pp * 512 + 512]
                    kb.op("vector", lambda e, src=src, sl=sl, xsl=xsl:
                          e.tensor_tensor(xsl, src[:, sl],
                                          src[:, sl], OP.mult),
                          reads=(B(srcb),), writes=(B(f"axbf{pp}"),))

                c_sq(0)
                for i in range(len(cjobs)):
                    j, srcb, src, hc, qq = cjobs[i]
                    pp = i % 2
                    xsl = xbf[:, pp * 512:pp * 512 + 512]
                    pbk = 6 + pp
                    kb.mm(PB[pbk][0:1, 0:512], c_ones[:, 0:1],
                          xsl, 512, B(f"pb{pbk}"),
                          start=True, stop=True,
                          reads=(B(f"axbf{pp}"), B("c_ones")))
                    if i + 1 < len(cjobs):
                        c_sq(i + 1)
                    kb.op("vector", lambda e, pbk=pbk, pp=pp:
                          e.tensor_reduce(
                        s_cden[0:1, 4 + pp:5 + pp], PB[pbk][0:1, 0:512],
                        AX, OP.max),
                        reads=(B(f"pb{pbk}"),), writes=(B("s_cden"),))
                    if hc == 0 and qq == 0:
                        kb.op("vector", lambda e, j=j, pp=pp:
                              e.tensor_copy(s_cden[0:1, j:j + 1],
                                            s_cden[0:1, 4 + pp:5 + pp]),
                              reads=(B("s_cden"),),
                              writes=(B("s_cden"),))
                    else:
                        kb.op("vector", lambda e, j=j, pp=pp:
                              e.tensor_tensor(
                                  s_cden[0:1, j:j + 1],
                                  s_cden[0:1, j:j + 1],
                                  s_cden[0:1, 4 + pp:5 + pp], OP.max),
                              reads=(B("s_cden"),),
                              writes=(B("s_cden"),))
                kb.op("vector", lambda e: e.tensor_tensor(
                    s_cden[0:1, 2:3], s_cden[0:1, 0:1], s_cden[0:1, 1:2],
                    OP.mult), reads=(B("s_cden"),), writes=(B("s_cden"),))
                kb.op("scalar", lambda e: e.activation(
                    s_cden[0:1, 2:3], s_cden[0:1, 2:3], AF.Ln),
                    reads=(B("s_cden"),), writes=(B("s_cden"),))
                kb.op("scalar", lambda e: e.activation(
                    s_cden[0:1, 2:3], s_cden[0:1, 2:3], AF.Exp,
                    scale=c_f32[0:1, 5:6]),
                    reads=(B("s_cden"), B("c_f32")), writes=(B("s_cden"),))
                kb.op("vector", lambda e: e.tensor_scalar(
                    s_cden[0:1, 3:4], s_cden[0:1, 2:3], -0.125, None,
                    OP.mult), reads=(B("s_cden"),), writes=(B("s_cden"),))
                kb.op("vector", lambda e: e.tensor_copy(
                    s_cbf[0:1, 0:1], s_cden[0:1, 3:4]),
                    reads=(B("s_cden"),), writes=(B("s_cbf"),))
                kb.mm(PB[7][0:128, 0:1], c_onesr[:],
                      s_cbf[0:1, 0:1], 1, B("pb7"),
                      start=True, stop=True,
                      reads=(B("s_cbf"), B("c_onesr")))
                kb.op("scalar", lambda e, b=b: e.copy(
                    s_negc[:, b:b + 1], PB[7][0:128, 0:1]),
                    reads=(B("pb7"),), writes=(B("s_negc"),))

                kb.op("vector", lambda e: e.tensor_copy(
                    xbf[:, 0:1], xbf[:, 0:1]),
                    reads=(B("axbf0"), B("axbf1")), writes=(B("axbf"),))
                del bufs["axbf0"]
                del bufs["axbf1"]
                kb.barrier()
                qkvstack.close()
                for n in (f"awq{b}", f"awk{b}", f"awv{b}"):
                    del bufs[n]
                    for ch in range(8):
                        del bufs[f"{n}c{ch}"]

                corestack = ExitStack()
                wob = mksbuf(corestack, f"awo{b}", [128, 8192], BF16)
                expb = [mksbuf(corestack, f"ae{i}_{b}", [128, 1024], BF16)
                        for i in range(2)]
                wv_bf = mksbuf(corestack, f"awvo{b}", [128, 8 * 512], BF16)
                den = mksbuf(corestack, f"aden{b}", [1, 1024], F32)
                recip = mksbuf(corestack, f"arc{b}", [1, 1024], F32)
                o_sb = mksbuf(corestack, f"ao{b}", [128, 2 * 512], F32)
                bufs[f"ao0_{b}"] = Buf(f"ao0_{b}")
                bufs[f"ao1_{b}"] = Buf(f"ao1_{b}")
                del bufs[f"ao{b}"]
                rb_sb = mksbuf(corestack, f"arb{b}", [128, 512], F32)
                wvu = mksbuf(corestack, f"awvu{b}", [128, 512], BF16)
                for ch in range(4):
                    kb.dma("gpsimd" if ch % 2 == 0 else "scalar",
                           atmp.ap().rearrange("p (c n) -> p c n", c=2),
                           d_wo.ap().rearrange("(c p) n -> p c n",
                                               p=128)[:, 2 * ch:2 * ch + 2,
                                                      :],
                           writes=(B("atmp"),))
                    kb.op("vector", lambda e, ch=ch, wob=wob:
                          e.tensor_copy(
                        wob[:, ch * 2048:(ch + 1) * 2048], atmp[:]),
                        reads=(B("atmp"),), writes=(B(f"awo{b}"),))

                pending_tail = [None]
                pending_tail_pe = [None]
                for qs in range(nqs):
                    nsc = 4 * qs + 4
                    for p in range(8):
                        def scores(sc):
                            par2 = sc % 2
                            m_d = sc - 4 * qs
                            u0 = 128 * m_d if m_d >= 0 else 0
                            t1, t2 = PB[2 * par2], PB[2 * par2 + 1]
                            b1 = B(f"pb{2 * par2}")
                            b2 = B(f"pb{2 * par2 + 1}")
                            eb = B(f"ae{par2}_{b}")
                            ept = expb[par2]
                            kb.mm(t1[:, u0:512],
                                  k_buf[0:64, p * S + sc * 128:
                                        p * S + (sc + 1) * 128],
                                  q_buf[0:64, p * S + qs * 512 + u0:
                                        p * S + (qs + 1) * 512],
                                  512 - u0, b1, start=True, stop=True,
                                  reads=(B("ak"), B("aq")),
                                  tile_position=(0, 0))
                            kb.mm(t2[:, u0:512],
                                  k_buf[64:128, p * S + sc * 128:
                                        p * S + (sc + 1) * 128],
                                  q_buf[64:128, p * S + qs * 512 + u0:
                                        p * S + (qs + 1) * 512],
                                  512 - u0, b2, start=True, stop=True,
                                  reads=(B("ak"), B("aq")),
                                  tile_position=(64, 0))
                            if u0 == 0:
                                kb.op("scalar", lambda e, par2=par2,
                                      ep=ept, b=b: e.activation(
                                          ep[:, 0:1024],
                                          PW[par2][:, 0:1024], AF.Exp,
                                          bias=s_negc[:, b:b + 1],
                                          scale=c_f32[:, 0:1]),
                                      reads=(b1, b2, B("s_negc"),
                                             B("c_f32")),
                                      writes=(eb,))
                            else:
                                kb.op("scalar", lambda e, u0=u0, par2=par2,
                                      ep=ept, b=b: e.activation(
                                          ep.rearrange(
                                              "p (c n) -> p c n",
                                              c=2)[:, :, u0:512],
                                          PW[par2].rearrange(
                                              "p (c n) -> p c n",
                                              c=2)[:, :, u0:512], AF.Exp,
                                          bias=s_negc[:, b:b + 1],
                                          scale=c_f32[:, 0:1]),
                                      reads=(b1, b2, B("s_negc"),
                                             B("c_f32")),
                                      writes=(eb,))
                            if m_d >= 0:
                                for off in (u0, 512 + u0):
                                    kb.op("vector", lambda e, off=off,
                                          ep=ept: e.tensor_tensor(
                                              ep[:, off:off + 128],
                                              ep[:, off:off + 128],
                                              c_mask[:], OP.mult),
                                          reads=(eb, B("c_mask")),
                                          writes=(eb,))

                        nvleft = [8]

                        def av_reg(sc):
                            par2 = sc % 2
                            m_d = sc - 4 * qs
                            u0 = 128 * m_d if m_d >= 0 else 0
                            lo = u0 + 128 if m_d >= 0 else 0
                            if lo >= 512:
                                return
                            eb = B(f"ae{par2}_{b}")
                            for hh in range(2):
                                kb.mm(PB[4 + hh][0:65, lo:512],
                                      v_buf[:, sc * 1040
                                            + (2 * p + hh) * 65:
                                            sc * 1040 + (2 * p + hh) * 65
                                            + 65],
                                      expb[par2][:, hh * 512 + lo:
                                                 hh * 512 + 512],
                                      512 - lo, B(f"pb{4 + hh}"),
                                      start=(sc == 0), stop=False,
                                      reads=(B("av"), eb))

                        def av_msk(sc):
                            if sc < 0:
                                return
                            m_d = sc - 4 * qs
                            if m_d < 0:
                                return
                            par2 = sc % 2
                            u0 = 128 * m_d
                            eb = B(f"ae{par2}_{b}")
                            for hh in range(2):
                                nvleft[0] -= 1
                                kb.mm(PB[4 + hh][0:65, u0:u0 + 128],
                                      v_buf[:, sc * 1040
                                            + (2 * p + hh) * 65:
                                            sc * 1040 + (2 * p + hh) * 65
                                            + 65],
                                      expb[par2][:, hh * 512 + u0:
                                                 hh * 512 + u0 + 128],
                                      128, B(f"pb{4 + hh}"),
                                      start=False, stop=(nvleft[0] < 2),
                                      reads=(B("av"), eb))

                        scores(0)
                        if pending_tail[0] is not None:
                            pending_tail[0]()
                            pending_tail[0] = None
                        for sc in range(1, nsc):
                            av_msk(sc - 2)
                            scores(sc)
                            av_reg(sc - 1)
                        av_reg(nsc - 1)
                        av_msk(nsc - 2)
                        av_msk(nsc - 1)
                        if pending_tail_pe[0] is not None:
                            pending_tail_pe[0]()
                            pending_tail_pe[0] = None

                        def tail_a(p=p):
                            kb.op("vector", lambda e, den=den:
                                  e.tensor_copy(
                                den[0:1, 0:512], PB[4][64:65, 0:512]),
                                reads=(B("pb4"),), writes=(B(f"aden{b}"),))
                            kb.op("vector", lambda e, den=den:
                                  e.tensor_copy(
                                den[0:1, 512:1024], PB[5][64:65, 0:512]),
                                reads=(B("pb5"),), writes=(B(f"aden{b}"),))
                            for hh in range(2):
                                kb.op("vector", lambda e, hh=hh, wvu=wvu:
                                      e.tensor_copy(
                                    wvu[hh * 64:(hh + 1) * 64, 0:512],
                                    PB[4 + hh][0:64, 0:512]),
                                    reads=(B(f"pb{4 + hh}"),),
                                    writes=(B(f"awvu{b}"),))
                            kb.op("scalar", lambda e, den=den:
                                  e.activation(den[:], den[:], AF.Ln),
                                reads=(B(f"aden{b}"),),
                                writes=(B(f"aden{b}"),))
                            kb.op("scalar", lambda e, den=den,
                                  recip=recip: e.activation(
                                recip[:], den[:], AF.Exp,
                                scale=c_f32[0:1, 3:4]),
                                reads=(B(f"aden{b}"), B("c_f32")),
                                writes=(B(f"arc{b}"),))

                        def tail_pe(p=p):
                            for hh in range(2):
                                kb.mm(PB[6][hh * 64:(hh + 1) * 64, 0:512],
                                      c_onesf[0:1, 0:64],
                                      recip[0:1, hh * 512:(hh + 1) * 512],
                                      512, B("pb6"),
                                      start=True, stop=True,
                                      reads=(B(f"arc{b}"), B("c_onesf")))
                            kb.op("vector", lambda e, rb_sb=rb_sb:
                                  e.tensor_copy(
                                rb_sb[:], PB[6][0:128, 0:512]),
                                reads=(B("pb6"),), writes=(B(f"arb{b}"),))
                            kb.op("vector", lambda e, p=p, wvu=wvu,
                                  wv_bf=wv_bf, rb_sb=rb_sb:
                                  e.tensor_tensor(
                                      wv_bf[:, p * 512:(p + 1) * 512],
                                      wvu[:, 0:512],
                                      rb_sb[:, 0:512],
                                      OP.mult),
                                  reads=(B(f"awvu{b}"), B(f"arb{b}")),
                                  writes=(B(f"awvo{b}"),))

                        pending_tail[0] = tail_a
                        pending_tail_pe[0] = tail_pe
                    pending_tail[0]()
                    pending_tail[0] = None
                    pending_tail_pe[0]()
                    pending_tail_pe[0] = None
                    for half in range(2):
                        kb.dma("sync",
                               x_in[half].ap().rearrange(
                                   "p (c n) -> p c n", c=8),
                               d_x1t[b, :, qs * 512 + half * 256:
                                     qs * 512 + (half + 1) * 256].rearrange(
                                   "(c p) n -> p c n", p=128),
                               writes=(B(f"ax{half}"),))
                    for dkg in range(2):
                        for hc in range(8):
                            for dkj in range(4):
                                dk = dkg * 4 + dkj
                                pj = dkg * 4 + dkj
                                kb.mm(PB[pj][:, 0:512],
                                      wob[:, hc * 1024 + dk * 128:
                                          hc * 1024 + (dk + 1) * 128],
                                      wv_bf[:, hc * 512:(hc + 1) * 512],
                                      512, B(f"pb{pj}"),
                                      start=(hc == 0), stop=(hc == 7),
                                      reads=(B(f"awo{b}"), B(f"awvo{b}")))
                        for dkj in range(4):
                            dk = dkg * 4 + dkj
                            pj = dkg * 4 + dkj
                            pt, pbn = PB[pj], B(f"pb{pj}")
                            par_o = dk % 2
                            osl = o_sb[:, par_o * 512:(par_o + 1) * 512]
                            aob = B(f"ao{par_o}_{b}")
                            kb.op("scalar", lambda e, pt=pt, dk=dk,
                                  osl=osl:
                                  e.activation(osl, pt[:, 0:512],
                                               AF.Identity,
                                               bias=c_bo[:, dk:dk + 1]),
                                  reads=(pbn, B("c_bo")),
                                  writes=(aob,))
                            for half in range(2):
                                kb.op("vector", lambda e, dk=dk, half=half,
                                      osl=osl:
                                      e.tensor_tensor(
                                          x_in[half][:, dk * 256:
                                                     (dk + 1) * 256],
                                          x_in[half][:, dk * 256:
                                                     (dk + 1) * 256],
                                          osl[:, half * 256:
                                              (half + 1) * 256],
                                          OP.add),
                                      reads=(B(f"ax{half}"), aob),
                                      writes=(B(f"ax{half}"),))
                    for half in range(2):
                        kb.dma("scalar",
                               d_x2t[b, :, qs * 512 + half * 256:
                                     qs * 512 + (half + 1) * 256].rearrange(
                                   "(c p) n -> p c n", p=128),
                               x_in[half].ap().rearrange(
                                   "p (c n) -> p c n", c=8),
                               reads=(B(f"ax{half}"),))
                if debug_outputs and b == 0:
                    kb.dma("sync", d_dbgq[:], q_buf[:, 0:8 * S],
                           reads=(B("aq"),))
                    kb.dma("sync", d_dbgk[:], k_buf[:], reads=(B("ak"),))
                    kb.dma("sync", d_dbgv[:], v_buf[:], reads=(B("av"),))
                    kb.dma("sync", d_dbgw[:], wv_bf[:],
                           reads=(B(f"awvo{b}"),))
                    kb.dma("sync", d_dbgc[:], s_negc[:],
                           reads=(B("s_negc"),))
                kb.barrier()
                corestack.close()
                for n in (f"awo{b}", f"ae0_{b}", f"ae1_{b}", f"awvo{b}",
                          f"aden{b}", f"arc{b}", f"ao0_{b}", f"ao1_{b}",
                          f"arb{b}", f"awvu{b}"):
                    del bufs[n]
            ph.close()
            for n in ("aq", "ak", "av", "ax0", "ax1", "alnx0", "alnx1",
                      "axbf", "axsq", "aab", "arw", "atmp", "abvb", "astg",
                      "astg0", "astg1"):
                del bufs[n]
            for n in [k2 for k2 in bufs if k2.startswith("alnx")]:
                del bufs[n]

        emit_ffn(d_xt, d_x1t, first=True, tag="f1")
        emit_attn()
        emit_ffn(d_x2t, d_out, first=False, tag="f3")

        with ExitStack() as semstack:
            for s in kb.all_sems:
                s.h = semstack.enter_context(nc.semaphore(s.name))
            block = top.enter_context(nc.Block())
            kb.replay(block)

    return nc, kb



def _layouts(L0_attn_inputs, L0_attn_W_q, L0_attn_b_q, L0_attn_W_k,
             L0_attn_b_k, L0_attn_W_v, L0_attn_b_v, L0_attn_W_o, L0_attn_b_o,
             L0_ffn_W_fc, L0_ffn_W_proj):
    f32 = np.float32
    xT = np.ascontiguousarray(
        np.asarray(L0_attn_inputs, f32).transpose(0, 2, 1))
    lay = dict(
        wfc=np.ascontiguousarray(np.asarray(L0_ffn_W_fc, f32).T),
        wpr=np.ascontiguousarray(np.asarray(L0_ffn_W_proj, f32).T),
        wq=np.ascontiguousarray(
            np.asarray(L0_attn_W_q, f32).transpose(1, 0, 2).reshape(D, D)),
        wk=np.ascontiguousarray(
            np.asarray(L0_attn_W_k, f32).transpose(1, 0, 2).reshape(D, D)),
        wv=np.ascontiguousarray(
            np.asarray(L0_attn_W_v, f32).transpose(1, 0, 2).reshape(D, D)),
        wo=np.ascontiguousarray(np.asarray(L0_attn_W_o, f32).T),
        bq=np.ascontiguousarray(np.asarray(L0_attn_b_q, f32).reshape(8, 128).T),
        bk=np.ascontiguousarray(np.asarray(L0_attn_b_k, f32).reshape(8, 128).T),
        bo=np.ascontiguousarray(np.asarray(L0_attn_b_o, f32).reshape(8, 128).T),
        bv=np.ascontiguousarray(np.asarray(L0_attn_b_v, f32).reshape(1, D)),
        mask=np.ascontiguousarray(
            np.triu(np.ones((128, 128))).astype(ml_dtypes.bfloat16)),
        ones=np.ones((128, 1), dtype=ml_dtypes.bfloat16),
        onesr=np.ones((1, 128), dtype=ml_dtypes.bfloat16),
        cf32=np.ascontiguousarray(
            np.tile(np.array([[0.125, 2.0, -1.2, -1.0, -0.5, 0.5]], f32),
                    (128, 1))),
        onesf=np.ones((1, 128), dtype=f32),
        onesc=np.ones((128, 1), dtype=f32),
    )
    return xT, lay


def kernel(**inputs):
    xT, lay = _layouts(**inputs)
    nc, _kb = build_nc(S_FULL)
    in_maps = []
    for c in range(NCORE):
        m = {"xt": np.ascontiguousarray(xT[c * BL:(c + 1) * BL])}
        m.update(lay)
        in_maps.append(m)
    res = run_bass_kernel_spmd(nc, in_maps, core_ids=list(range(NCORE)))
    outs = [res.results[c]["out"] for c in range(NCORE)]
    full = np.concatenate(outs, axis=0)
    return np.ascontiguousarray(full.transpose(0, 2, 1)).astype(np.float32)


# version 32 (final)
# speedup vs baseline: 1.0093x; 1.0093x over previous
import bisect
import numpy as np
import ml_dtypes
from contextlib import ExitStack

import concourse.bass as bass
import concourse.mybir as mybir
from concourse.bass_utils import run_bass_kernel_spmd

F32 = mybir.dt.float32
BF16 = mybir.dt.bfloat16
AF = mybir.ActivationFunctionType
OP = mybir.AluOpType
AX = mybir.AxisListType.X

ENGINES = ("sync", "tensor", "vector", "scalar", "gpsimd")

B_TOT, S_FULL, D, H, HD = 16, 2048, 1024, 16, 64
NCORE = 8
BL = B_TOT // NCORE
F4 = 4 * D



class _Sem:
    __slots__ = ("name", "n", "h", "_last")

    def __init__(self, name):
        self.name = name
        self.n = 0
        self.h = None
        self._last = 0


class _Op:
    __slots__ = ("fn", "waits", "incs", "seq")

    def __init__(self, fn, seq):
        self.fn = fn
        self.waits = []
        self.incs = []
        self.seq = seq


class _Eng:
    def __init__(self, name):
        self.name = name
        self.ops = []
        self.sem = _Sem("S" + name[:2])
        self.inc_seqs = []
        self.inc_vals = []
        self.waited = {}


class Buf:
    __slots__ = ("w", "r", "name", "bank")

    def __init__(self, name="", bank=None):
        self.name = name
        self.w = None
        self.r = {}
        self.bank = bank


class KB:
    def __init__(self):
        self.eng = {n: _Eng(n) for n in ENGINES}
        self.dma_sems = {}
        self.all_sems = [e.sem for e in self.eng.values()]
        self.dr_cols = 0
        self.dr_banks = set()
        self.n_drains = 0
        self.n_mm = 0
        self.hist = []

    def _resolve(self, mk):
        if mk[0] == "s":
            return mk[1], mk[2]
        _, engname, seq = mk
        e = self.eng[engname]
        if e.ops[seq].incs:
            return e.ops[seq].incs[0]
        i = bisect.bisect_left(e.inc_seqs, seq)
        if i < len(e.inc_seqs):
            return e.sem, e.inc_vals[i]
        e.sem.n += 1
        e.ops[seq].incs.append((e.sem, e.sem.n))
        e.inc_seqs.append(seq)
        e.inc_vals.append(e.sem.n)
        return e.sem, e.sem.n

    def _add_wait(self, engname, op, mk):
        if mk[0] == "e" and mk[1] == engname and engname == "tensor":
            return
        sem, val = self._resolve(mk)
        e = self.eng[engname]
        if e.waited.get(sem.name, -1) >= val:
            return
        op.waits.append((sem, val))
        e.waited[sem.name] = val

    def op(self, engname, fn, reads=(), writes=()):
        e = self.eng[engname]
        o = _Op(fn, len(e.ops))
        e.ops.append(o)
        e.sem.n += 1
        o.incs.append((e.sem, e.sem.n))
        e.inc_seqs.append(o.seq)
        e.inc_vals.append(e.sem.n)
        for b in reads:
            if b.w is not None:
                self._add_wait(engname, o, b.w)
        for b in writes:
            if b.w is not None:
                self._add_wait(engname, o, b.w)
            for rm in b.r.values():
                self._add_wait(engname, o, rm)
        m = ("e", engname, o.seq)
        for b in reads:
            b.r[engname] = m
        for b in writes:
            b.w = m
            b.r = {}
        return o

    def dma(self, q, dst_ap, src_ap, reads=(), writes=()):
        pool = self.dma_sems.get(q)
        if pool is None:
            pool = [_Sem(f"D{q[:2]}{j}") for j in range(4)]
            self.dma_sems[q] = pool
            self.all_sems.extend(pool)
            self._dma_idx = getattr(self, "_dma_idx", {})
            self._dma_idx[q] = 0
        idx = self._dma_idx[q] % 4
        self._dma_idx[q] += 1
        dsem = pool[idx]
        e = self.eng[q]
        o = _Op(lambda eng, d=dst_ap, s=src_ap: eng.dma_start(out=d, in_=s),
                len(e.ops))
        e.ops.append(o)
        if dsem.n > 0 and e.waited.get(dsem.name, -1) < dsem.n:
            o.waits.append((dsem, dsem.n))
            e.waited[dsem.name] = dsem.n
        for b in reads:
            if b.w is not None:
                self._add_wait(q, o, b.w)
        for b in writes:
            if b.w is not None:
                self._add_wait(q, o, b.w)
            for rm in b.r.values():
                self._add_wait(q, o, rm)
        dsem.n += 16
        o.incs.append((dsem, dsem.n))
        m = ("s", dsem, dsem.n)
        for b in reads:
            b.r["#" + q] = m
        for b in writes:
            b.w = m
            b.r = {}
        return o

    def _maybe_drain(self, bank, ncols):
        self.hist.append(bank)
        if len(self.hist) > 8:
            self.hist.pop(0)
        if len(self.hist) == 8 and len(set(self.hist)) >= 4:
            self.dr_cols = 0
            self.dr_banks = set()
            return
        banks = self.dr_banks | {bank}
        thresh = 1536 * min(len(banks), 4)
        if self.dr_cols + ncols > thresh:
            e = self.eng["tensor"]
            e.ops.append(_Op(lambda eng: eng.drain(), len(e.ops)))
            self.n_drains += 1
            self.dr_cols = 0
            self.dr_banks = set()
            self.hist = []

    def mm(self, out_ap, lhsT_ap, rhs_ap, ncols, psum_buf, start, stop,
           reads=(), tile_position=None):
        self._maybe_drain(psum_buf.bank, ncols)
        self.dr_cols += ncols
        self.dr_banks.add(psum_buf.bank)
        self.n_mm += 1
        if tile_position is None:
            fn = (lambda eng, o=out_ap, l=lhsT_ap, r=rhs_ap, st=start,
                  sp=stop: eng.matmul(o, l, r, start=st, stop=sp))
        else:
            fn = (lambda eng, o=out_ap, l=lhsT_ap, r=rhs_ap, st=start,
                  sp=stop, tp=tile_position:
                  eng.matmul(o, l, r, start=st, stop=sp, tile_position=tp))
        return self.op("tensor", fn, reads=reads, writes=(psum_buf,))

    def barrier(self, dma=True):
        marks = []
        for e in self.eng.values():
            if e.ops:
                marks.append(("e", e.name, len(e.ops) - 1))
        if dma:
            for pool in self.dma_sems.values():
                for dsem in pool:
                    if dsem.n:
                        marks.append(("s", dsem, dsem.n))
        resolved = [self._resolve(mk) for mk in marks]
        for e in self.eng.values():
            o = _Op(lambda eng: eng.nop(), len(e.ops))
            e.ops.append(o)
            for sem, val in resolved:
                if e.waited.get(sem.name, -1) < val:
                    o.waits.append((sem, val))
                    e.waited[sem.name] = val
        self.dr_cols = 0
        self.dr_banks = set()

    def replay(self, block):
        kb = self

        def mk_body(engname):
            def body(eng):
                for o in kb.eng[engname].ops:
                    for sem, val in o.waits:
                        eng.wait_ge(sem.h, val)
                    inst = o.fn(eng)
                    for sem, val in o.incs:
                        inst.then_inc(sem.h, val - sem._last)
                        sem._last = val
                for q, pool in kb.dma_sems.items():
                    if q == engname:
                        for dsem in pool:
                            if dsem.n > 0:
                                eng.wait_ge(dsem.h, dsem.n)
            return body

        for engname in ENGINES:
            getattr(block, engname)(mk_body(engname))



def build_nc(S=S_FULL, detect_races=False, debug_outputs=False):
    assert S % 512 == 0
    nst = S // 256
    nqs = S // 512
    QCOLS = max(8 * S, 16384)

    nc = bass.Bass("TRN2", target_bir_lowering=False,
                   detect_race_conditions=detect_races)

    d_xt = nc.dram_tensor("xt", [BL, D, S], F32, kind="ExternalInput")
    d_wfc = nc.dram_tensor("wfc", [D, F4], F32, kind="ExternalInput")
    d_wpr = nc.dram_tensor("wpr", [F4, D], F32, kind="ExternalInput")
    d_wq = nc.dram_tensor("wq", [D, D], F32, kind="ExternalInput")
    d_wk = nc.dram_tensor("wk", [D, D], F32, kind="ExternalInput")
    d_wv = nc.dram_tensor("wv", [D, D], F32, kind="ExternalInput")
    d_wo = nc.dram_tensor("wo", [D, D], F32, kind="ExternalInput")
    d_bq = nc.dram_tensor("bq", [128, 8], F32, kind="ExternalInput")
    d_bk = nc.dram_tensor("bk", [128, 8], F32, kind="ExternalInput")
    d_bo = nc.dram_tensor("bo", [128, 8], F32, kind="ExternalInput")
    d_bv = nc.dram_tensor("bv", [1, D], F32, kind="ExternalInput")
    d_mask = nc.dram_tensor("mask", [128, 128], BF16, kind="ExternalInput")
    d_ones = nc.dram_tensor("ones", [128, 1], BF16, kind="ExternalInput")
    d_onesr = nc.dram_tensor("onesr", [1, 128], BF16, kind="ExternalInput")
    d_onesf = nc.dram_tensor("onesf", [1, 128], F32, kind="ExternalInput")
    d_onesc = nc.dram_tensor("onesc", [128, 1], F32, kind="ExternalInput")
    d_cf32 = nc.dram_tensor("cf32", [128, 6], F32, kind="ExternalInput")
    d_out = nc.dram_tensor("out", [BL, D, S], F32, kind="ExternalOutput")
    scratch_kind = dict(kind="ExternalOutput") if debug_outputs else {}
    if debug_outputs:
        d_dbgq = nc.dram_tensor("dbgq", [128, 8 * S], BF16, kind="ExternalOutput")
        d_dbgk = nc.dram_tensor("dbgk", [128, 8 * S], BF16, kind="ExternalOutput")
        d_dbgv = nc.dram_tensor("dbgv", [128, (S // 128) * 1040], BF16,
                                kind="ExternalOutput")
        d_dbgw = nc.dram_tensor("dbgw", [128, 8 * 512], BF16,
                                kind="ExternalOutput")
        d_dbgc = nc.dram_tensor("dbgc", [128, BL], F32, kind="ExternalOutput")
    d_x1t = nc.dram_tensor("x1t", [BL, D, S], F32, **scratch_kind)
    d_x2t = nc.dram_tensor("x2t", [BL, D, S], F32, **scratch_kind)
    d_wfcb = nc.dram_tensor("wfcb", [D, F4], BF16)
    d_wprb = nc.dram_tensor("wprb", [F4, D], BF16)
    d_cfl = nc.dram_tensor("cfl", [1, 256], F32)

    kb = KB()
    top = ExitStack()
    bufs = {}

    def mksbuf(stack, name, shape, dt):
        t = stack.enter_context(nc.sbuf_tensor(name, shape, dt))
        bufs[name] = Buf(name)
        return t

    def B(n):
        return bufs[n]

    with top:
        c_bq = mksbuf(top, "c_bq", [128, 8], F32)
        c_bk = mksbuf(top, "c_bk", [128, 8], F32)
        c_bo = mksbuf(top, "c_bo", [128, 8], F32)
        c_mask = mksbuf(top, "c_mask", [128, 128], BF16)
        c_ones = mksbuf(top, "c_ones", [128, 1], BF16)
        c_onesr = mksbuf(top, "c_onesr", [1, 128], BF16)
        c_onesf = mksbuf(top, "c_onesf", [1, 128], F32)
        c_onesc = mksbuf(top, "c_onesc", [128, 1], F32)
        c_f32 = mksbuf(top, "c_f32", [128, 6], F32)
        s_negc = mksbuf(top, "s_negc", [128, BL], F32)
        s_cmax = mksbuf(top, "s_cmax", [128, 2], F32)
        s_crow = mksbuf(top, "s_crow", [1, 256], F32)
        s_cden = mksbuf(top, "s_cden", [1, 8], F32)
        s_cbf = mksbuf(top, "s_cbf", [1, 4], BF16)
        bufs["d_cfl"] = Buf("d_cfl")
        PW = []
        for j in range(4):
            t = top.enter_context(nc.psum_tensor(f"pw{j}", [128, 1024], F32))
            PW.append(t)
        PB = []
        for i in range(8):
            bufs[f"pb{i}"] = Buf(f"pb{i}", bank=i)
            PB.append(PW[i // 2][:, (i % 2) * 512:(i % 2) * 512 + 512])

        for nm, t, dd in (("c_bq", c_bq, d_bq), ("c_bk", c_bk, d_bk),
                          ("c_bo", c_bo, d_bo), ("c_mask", c_mask, d_mask),
                          ("c_ones", c_ones, d_ones),
                          ("c_onesr", c_onesr, d_onesr),
                          ("c_onesf", c_onesf, d_onesf),
                          ("c_onesc", c_onesc, d_onesc),
                          ("c_f32", c_f32, d_cf32)):
            kb.dma("gpsimd", t[:], dd[:], writes=(B(nm),))

        def kb_mms(lst):
            for o, l, rr, nc_, pb, st_, sp, rd in lst:
                kb.mm(o, l, rr, nc_, pb, start=st_, stop=sp, reads=rd)

        def emit_ln(xt_t, xn, lnx_t, lnxn, xsq_t, xsqn,
                    ab_t, abn, rows_t, rowsn, tmp_t, tmpn):
            a1m, a2p, a2m, bpre, bmms, bpost, c = emit_ln_parts(
                xt_t, xn, lnx_t, lnxn, xsq_t, xsqn, ab_t, abn,
                rows_t, rowsn, tmp_t, tmpn)
            kb_mms(a1m()); a2p("vector"); kb_mms(a2m()); bpre()
            kb_mms(bmms()); bpost(); c()

        def emit_ln_parts(xt_t, xn, lnx_t, lnxn, xsq_t, xsqn,
                          ab_t, abn, rows_t, rowsn, tmp_t, tmpn):
            xB, lnxB = B(xn), B(lnxn)
            xsqB, abB, rowsB, tmpB = B(xsqn), B(abn), B(rowsn), B(tmpn)
            p6B, p7B = B("pb6"), B("pb7")

            def a1_mms():
              return [(PB[6][0:1, 0:256], c_onesc[:],
                       xt_t[:, dk * 256:(dk + 1) * 256], 256, p6B,
                       dk == 0, dk == 7, (xB, B("c_onesc")))
                      for dk in range(8)]

            def a2_pre(eng="scalar"):
              if eng == "scalar":
                  kb.op("scalar", lambda e, a=xsq_t, b_=xt_t: e.activation(
                    a[:, 0:2048], b_[:, 0:2048], AF.Square),
                      reads=(xB,), writes=(xsqB,))
              else:
                  kb.op(eng, lambda e, a=xsq_t, b_=xt_t:
                        e.tensor_tensor(
                    a[:, 0:2048], b_[:, 0:2048], b_[:, 0:2048], OP.mult),
                      reads=(xB,), writes=(xsqB,))

            def a2_mms():
              return [(PB[7][0:1, 0:256], c_ones[:, 0:1],
                       xsq_t[:, dk * 256:(dk + 1) * 256], 256, p7B,
                       dk == 0, dk == 7, (xsqB, B("c_ones")))
                      for dk in range(8)]
            def b_pre():
              r = rows_t
              kb.op("vector", lambda e: e.tensor_scalar(
                  r[0:1, 0:256], PB[6][0:1, 0:256], 1.0 / 1024, None, OP.mult),
                  reads=(p6B,), writes=(rowsB,))
              kb.op("vector", lambda e: e.tensor_scalar(
                  r[0:1, 256:512], PB[7][0:1, 0:256], 1.0 / 1024, None,
                  OP.mult), reads=(p7B,), writes=(rowsB,))
              kb.op("vector", lambda e: e.tensor_tensor(
                  r[0:1, 512:768], r[0:1, 0:256], r[0:1, 0:256], OP.mult),
                  reads=(rowsB,), writes=(rowsB,))
              kb.op("vector", lambda e: e.tensor_tensor(
                  r[0:1, 256:512], r[0:1, 256:512], r[0:1, 512:768],
                  OP.subtract), reads=(rowsB,), writes=(rowsB,))
              kb.op("vector", lambda e: e.tensor_scalar(
                  r[0:1, 256:512], r[0:1, 256:512], 1e-5, None, OP.add),
                  reads=(rowsB,), writes=(rowsB,))
              kb.op("scalar", lambda e, r=r: e.activation(
                  r[0:1, 256:512], r[0:1, 256:512], AF.Ln),
                  reads=(rowsB,), writes=(rowsB,))
              kb.op("scalar", lambda e, r=r: e.activation(
                  r[0:1, 256:512], r[0:1, 256:512], AF.Exp,
                  scale=c_f32[0:1, 4:5]),
                  reads=(rowsB, B("c_f32")), writes=(rowsB,))
              kb.op("vector", lambda e, r=r: e.tensor_tensor(
                  r[0:1, 0:256], r[0:1, 0:256], r[0:1, 256:512], OP.mult),
                  reads=(rowsB,), writes=(rowsB,))

            def b_mms():
              r = rows_t
              return [
                  (PB[6][0:128, 0:256], c_onesf[:], r[0:1, 256:512], 256,
                   p6B, True, True, (rowsB, B("c_onesf"))),
                  (PB[7][0:128, 0:256], c_onesf[:], r[0:1, 0:256], 256,
                   p7B, True, True, (rowsB, B("c_onesf")))]

            def b_post():
              kb.op("vector", lambda e, a=ab_t: e.tensor_copy(
                  a[:, 0:512].rearrange("p (c n) -> p c n", c=2),
                  PW[3].rearrange("p (c n) -> p c n", c=4)[:, 0::2, 0:256]),
                  reads=(p6B, p7B), writes=(abB,))

            def stage_c(d0=0, d1=8):
              for dk in range(d0, d1):
                  sl = slice(dk * 256, (dk + 1) * 256)
                  lnxdB = bufs.setdefault(lnxn + f"d{dk}",
                                          Buf(lnxn + f"d{dk}"))
                  kb.op("vector", lambda e, sl=sl: e.tensor_tensor(
                      tmp_t[:, sl], xt_t[:, sl], ab_t[:, 0:256], OP.mult),
                      reads=(xB, abB), writes=(tmpB,))
                  kb.op("vector", lambda e, sl=sl: e.tensor_tensor(
                      lnx_t[:, sl], tmp_t[:, sl], ab_t[:, 256:512],
                      OP.subtract), reads=(tmpB, abB),
                      writes=(lnxB, lnxdB))
            return a1_mms, a2_pre, a2_mms, b_pre, b_mms, \
                b_post, stage_c

        def emit_ffn(src_dram, dst_dram, first, tag):
            ph = ExitStack()
            wfc = mksbuf(ph, tag + "wfc", [128, 8 * F4], BF16)
            wpr = mksbuf(ph, tag + "wpr", [128, 32 * D], BF16)
            for j in range(8):
                bufs[tag + f"wfg{j}"] = Buf(tag + f"wfg{j}")
                bufs[tag + f"wpr{j}"] = Buf(tag + f"wpr{j}")
                bufs[tag + f"wfcc{j}"] = Buf(tag + f"wfcc{j}")
                bufs[tag + f"wprc{j}"] = Buf(tag + f"wprc{j}")
            if first:
              with ExitStack() as st2:
                stg = mksbuf(st2, tag + "wst", [128, 2 * F4], F32)
                bufs[tag + "wst0"] = Buf(tag + "wst0")
                bufs[tag + "wst1"] = Buf(tag + "wst1")
                wbacks = []
                for dk in range(8):
                    sl = slice((dk % 2) * F4, (dk % 2 + 1) * F4)
                    stB = B(tag + f"wst{dk % 2}")
                    kb.dma(("gpsimd", "scalar", "sync")[dk % 3],
                           stg[:, sl],
                           d_wfc.ap().rearrange("(c p) n -> p c n",
                                                p=128)[:, dk, :],
                           writes=(stB,))
                    eng = "vector" if dk % 2 == 0 else "scalar"
                    kb.op(eng, lambda e, a=wfc[:, dk * F4:(dk + 1) * F4],
                          s=stg[:, sl], eng=eng:
                          (e.tensor_copy(a, s) if eng == "vector"
                           else e.copy(a, s)),
                          reads=(stB,), writes=(B(tag + f"wfcc{dk}"),))
                    wbacks.append((d_wfcb.ap().rearrange(
                        "(c p) n -> p c n", p=128)[:, dk, :],
                        wfc[:, dk * F4:(dk + 1) * F4],
                        B(tag + f"wfcc{dk}")))
                for g in range(8):
                    sl = slice((g % 2) * F4, (g % 2 + 1) * F4)
                    stB = B(tag + f"wst{g % 2}")
                    kb.dma(("scalar", "sync", "gpsimd")[g % 3],
                           stg[:, sl].rearrange("p (c n) -> p c n", c=4),
                           d_wpr.ap().rearrange(
                               "(c p) n -> p c n",
                               p=128)[:, 4 * g:4 * g + 4, :],
                           writes=(stB,))
                    eng = "vector" if g % 2 == 0 else "scalar"
                    kb.op(eng, lambda e, a=wpr[:, g * F4:(g + 1) * F4],
                          s=stg[:, sl], eng=eng:
                          (e.tensor_copy(a, s) if eng == "vector"
                           else e.copy(a, s)),
                          reads=(stB,), writes=(B(tag + f"wprc{g}"),))
                    wbacks.append((d_wprb.ap().rearrange(
                        "(c p) n -> p c n",
                        p=128)[:, 4 * g:4 * g + 4, :],
                        wpr[:, g * F4:(g + 1) * F4].rearrange(
                            "p (c n) -> p c n", c=4),
                        B(tag + f"wprc{g}")))
                for dst, src, rb in wbacks:
                    kb.dma("gpsimd", dst, src, reads=(rb,))
                kb.barrier(dma=False)
                del bufs[tag + "wst"]
                del bufs[tag + "wst0"]
                del bufs[tag + "wst1"]
            else:
                for j in range(8):
                    kb.dma("gpsimd",
                           wfc.ap().rearrange("p (c n) -> p c n",
                                              c=8)[:, :, j * 512:
                                                   (j + 1) * 512],
                           d_wfcb.ap().rearrange(
                               "(c p) n -> p c n",
                               p=128)[:, :, j * 512:(j + 1) * 512],
                           writes=(B(tag + f"wfg{j}"),))
                    kb.dma("scalar",
                           wpr[:, j * F4:(j + 1) * F4].rearrange(
                               "p (c n) -> p c n", c=4),
                           d_wprb.ap().rearrange(
                               "(c p) n -> p c n",
                               p=128)[:, 4 * j:4 * j + 4, :],
                           writes=(B(tag + f"wpr{j}"),))

            x_in = [mksbuf(ph, f"{tag}x{i}", [128, 2048], F32)
                    for i in range(3)]
            xbf = mksbuf(ph, tag + "xbf", [128, 2048], BF16)
            xsq = mksbuf(ph, tag + "xsq", [128, 2048], BF16)
            lnx = mksbuf(ph, tag + "lnx", [128, 2 * 2048], BF16)
            bufs[tag + "lnx0"] = Buf(tag + "lnx0")
            bufs[tag + "lnx1"] = Buf(tag + "lnx1")
            del bufs[tag + "lnx"]
            h_sb = mksbuf(ph, tag + "h", [128, 2 * 1024], BF16)
            bufs[tag + "h0"] = Buf(tag + "h0")
            bufs[tag + "h1"] = Buf(tag + "h1")
            del bufs[tag + "h"]
            yb = mksbuf(ph, tag + "y", [128, 2048], F32)
            mb = mksbuf(ph, tag + "m", [128, 2048], F32)
            ab = mksbuf(ph, tag + "ab", [128, 512], F32)
            rows = mksbuf(ph, tag + "rw", [1, 768], F32)

            tiles = [(b, st) for b in range(BL) for st in range(nst)]
            nt = len(tiles)

            def load_tile(i):
                b, st = tiles[i]
                par = i % 3
                kb.dma("sync",
                       x_in[par].ap().rearrange("p (c n) -> p c n", c=8),
                       src_dram[b, :, st * 256:(st + 1) * 256].rearrange(
                           "(c p) n -> p c n", p=128),
                       writes=(B(f"{tag}x{par}"),))

            def ln_stages(i):
                par = i % 3
                lp = i % 2
                return emit_ln_parts(
                    x_in[par], f"{tag}x{par}",
                    lnx[:, lp * 2048:(lp + 1) * 2048], tag + f"lnx{lp}",
                    xsq, tag + "xsq", ab, tag + "ab",
                    rows, tag + "rw", mb, tag + "m")

            for j in range(min(3, nt)):
                load_tile(j)
            a1m, a2p, a2m, bpre_f, bmms_f, bpost_f, c_f = ln_stages(0)
            kb_mms(a1m()); a2p("vector"); kb_mms(a2m()); bpre_f()
            kb_mms(bmms_f()); bpost_f(); c_f()
            pending_spike = [None]


            for i in range(nt):
                b, st = tiles[i]
                par = i % 3
                xb, xt = B(f"{tag}x{par}"), x_in[par]
                hB = (B(tag + "h0"), B(tag + "h1"))
                yB, mB = B(tag + "y"), B(tag + "m")
                zB, ZB = B(tag + "xbf"), B(tag + "xsq")

                def h_mms(r):
                    out = []
                    lp = i % 2
                    lnxv = lnx[:, lp * 2048:(lp + 1) * 2048]
                    for pair in range(2):
                        for dk in range(8):
                            lnxdB = bufs.setdefault(
                                tag + f"lnx{lp}d{dk}",
                                Buf(tag + f"lnx{lp}d{dk}"))
                            for fr in (2 * pair, 2 * pair + 1):
                                fk = r * 4 + fr
                                pt, pc = PB[pair], (fr % 2) * 256
                                out.append((pt[:, pc:pc + 256],
                                            wfc[:, dk * F4 + fk * 128:
                                                dk * F4 + (fk + 1) * 128],
                                            lnxv[:, dk * 256:
                                                 (dk + 1) * 256],
                                            256, B(f"pb{pair}"),
                                            dk == 0 and fr % 2 == 0,
                                            dk == 7 and fr % 2 == 1,
                                            (B(tag + f"wfg{r}"), lnxdB)))
                    return out

                def y_mms(r):
                    out = []
                    slot = r % 2
                    for fr in range(4):
                        fk = r * 4 + fr
                        for dk in range(8):
                            pt = PB[2 + dk // 2]
                            pc = (dk % 2) * 256
                            out.append((pt[:, pc:pc + 256],
                                        wpr[:, fk * D + dk * 128:
                                            fk * D + (dk + 1) * 128],
                                        h_sb[:, slot * 1024 + fr * 256:
                                             slot * 1024 + (fr + 1) * 256],
                                        256, B(f"pb{2 + dk // 2}"),
                                        r == 0 and fr == 0 and dk % 2 == 0,
                                        r == 7 and fr == 3 and dk % 2 == 1,
                                        (B(tag + f"wpr{r}"), hB[fr // 2])))
                    return out

                def emit_mms(a, bl, extra=(), pre=4, front=False):
                    merged = []
                    ia = ib = 0
                    for j in range(min(pre, len(a))):
                        merged.append(a[ia]); ia += 1
                    while ia < len(a) or ib < len(bl):
                        if ib < len(bl):
                            merged.append(bl[ib]); ib += 1
                        if ia < len(a):
                            merged.append(a[ia]); ia += 1
                    if extra:
                        span = len(merged) // 2 if front else len(merged)
                        step = max(1, span // len(extra))
                        woven, ei = [], 0
                        for j, m in enumerate(merged):
                            woven.append(m)
                            if ei < len(extra) and j % step == step - 1:
                                woven.append(extra[ei]); ei += 1
                        woven.extend(extra[ei:])
                        merged = woven
                    kb_mms(merged)

                def relu_round(r):
                    slot = r % 2
                    for pair in range(2):
                        dst = h_sb[:, slot * 1024 + pair * 512:
                                   slot * 1024 + (pair + 1) * 512]
                        kb.op("scalar", lambda e, a=dst, pair=pair:
                              e.activation(
                            a, PW[0][:, pair * 512:(pair + 1) * 512],
                            AF.Relu),
                            reads=(B(f"pb{pair}"),), writes=(hB[pair],))
                        kb.op("scalar", lambda e, a=dst: e.activation(
                            a, a, AF.Square),
                            reads=(hB[pair],), writes=(hB[pair],))

                stages = ln_stages(i + 1) if i + 1 < nt else None
                emit_mms(h_mms(0), [])
                relu_round(0)
                for r in range(1, 8):
                    extra = ()
                    if stages is not None:
                        if r == 1:
                            extra = stages[0]()
                        elif r == 6:
                            extra = stages[2]()
                        elif r == 7:
                            extra = stages[4]()
                    emit_mms(h_mms(r), y_mms(r - 1), extra,
                             front=(r in (6, 7)))
                    relu_round(r)
                    if pending_spike[0] and r <= 5:
                        k_n = max(1, -(-len(pending_spike[0]) // (6 - r)))
                        for t_ in pending_spike[0][:k_n]:
                            t_()
                        del pending_spike[0][:k_n]
                        if not pending_spike[0]:
                            pending_spike[0] = None
                    if stages is not None:
                        if r == 5:
                            stages[1]("vector")
                        elif r == 6:
                            stages[3]()
                        elif r == 7:
                            stages[5]()
                            stages[6]()
                emit_mms([], y_mms(7))
                if pending_spike[0]:
                    for t_ in pending_spike[0]:
                        t_()
                    pending_spike[0] = None
                for half2 in range(2):
                    kb.op("vector", lambda e, half2=half2, yb=yb:
                          e.tensor_scalar(
                        yb[:, half2 * 1024:(half2 + 1) * 1024],
                        PW[1 + half2][:, 0:1024], 2.0, -1.2,
                        OP.mult, OP.add),
                        reads=(B(f"pb{2 + 2 * half2}"),
                               B(f"pb{3 + 2 * half2}")),
                        writes=(yB,))

                def spike_thunks(i=i, b=b, st=st, xb=xb, xt=xt):
                    th = []
                    th.append(lambda: kb.op("vector", lambda e, mb=mb,
                        yb=yb: e.tensor_scalar(
                        mb[:], yb[:], -0.8, None, OP.add),
                        reads=(yB,), writes=(mB,)))
                    th.append(lambda: kb.op("scalar", lambda e, xbf=xbf,
                        mb=mb: e.activation(xbf[:], mb[:], AF.Sign),
                        reads=(mB,), writes=(zB,)))
                    th.append(lambda: kb.op("vector", lambda e, xsq=xsq,
                        xbf=xbf: e.tensor_copy(xsq[:], xbf[:]),
                        reads=(zB,), writes=(ZB,)))
                    for _t in range(4):
                        th.append(lambda: kb.op("vector", lambda e, mb=mb,
                            xbf=xbf: e.scalar_tensor_tensor(
                            mb[:], mb[:], 0.9, xbf[:], OP.mult,
                            OP.subtract),
                            reads=(mB, zB), writes=(mB,)))
                        th.append(lambda: kb.op("vector", lambda e, mb=mb,
                            yb=yb: e.tensor_tensor(
                            mb[:], mb[:], yb[:], OP.add),
                            reads=(mB, yB), writes=(mB,)))
                        th.append(lambda: kb.op("scalar", lambda e, xbf=xbf,
                            mb=mb: e.activation(xbf[:], mb[:], AF.Sign),
                            reads=(mB,), writes=(zB,)))
                        th.append(lambda: kb.op("vector", lambda e, xsq=xsq,
                            xbf=xbf: e.tensor_tensor(
                            xsq[:], xsq[:], xbf[:], OP.add),
                            reads=(ZB, zB), writes=(ZB,)))
                    th.append(lambda: kb.op("vector", lambda e, mb=mb,
                        xsq=xsq: e.tensor_scalar(
                        mb[:], xsq[:], 0.05, 0.25, OP.mult, OP.add),
                        reads=(ZB,), writes=(mB,)))
                    th.append(lambda: kb.op("vector", lambda e, xt=xt,
                        mb=mb: e.tensor_tensor(
                        xt[:], xt[:], mb[:], OP.add),
                        reads=(xb, mB), writes=(xb,)))
                    th.append(lambda: kb.dma(
                        "sync", dst_dram[b, :, st * 256:(st + 1) * 256]
                        .rearrange("(c p) n -> p c n", p=128),
                        xt.ap().rearrange("p (c n) -> p c n", c=8),
                        reads=(xb,)))
                    if i + 3 < nt:
                        th.append(lambda: load_tile(i + 3))
                    return th

                if i + 1 < nt:
                    pending_spike[0] = spike_thunks()
                else:
                    for t_ in spike_thunks():
                        t_()
            kb.barrier()
            ph.close()
            for n in ("wfc", "wpr", "x0", "x1", "x2", "xbf", "xsq",
                      "lnx0", "lnx1", "h0", "h1", "y", "m", "ab", "rw"):
                del bufs[tag + n]
            for n in [k2 for k2 in bufs if k2.startswith(tag + "lnx")]:
                del bufs[n]
            for j in range(8):
                del bufs[tag + f"wfg{j}"]
                del bufs[tag + f"wpr{j}"]
                del bufs[tag + f"wfcc{j}"]
                del bufs[tag + f"wprc{j}"]

        def emit_attn():
            ph = ExitStack()
            q_buf = mksbuf(ph, "aq", [128, QCOLS], BF16)
            k_buf = mksbuf(ph, "ak", [128, 8 * S], BF16)
            v_buf = mksbuf(ph, "av", [128, (S // 128) * 1040], BF16)
            x_in = [mksbuf(ph, f"ax{i}", [128, 2048], F32) for i in range(2)]
            lnx = mksbuf(ph, "alnx", [128, 2 * 2048], BF16)
            bufs["alnx0"] = Buf("alnx0")
            bufs["alnx1"] = Buf("alnx1")
            del bufs["alnx"]
            xbf = mksbuf(ph, "axbf", [128, 2048], BF16)
            xsq = mksbuf(ph, "axsq", [128, 2048], BF16)
            ab = mksbuf(ph, "aab", [128, 512], F32)
            rows = mksbuf(ph, "arw", [1, 768], F32)
            atmp = mksbuf(ph, "atmp", [128, 2048], F32)
            astg = mksbuf(ph, "astg", [128, 2048], F32)
            bufs["astg0"] = Buf("astg0")
            bufs["astg1"] = Buf("astg1")
            c_bvb = mksbuf(ph, "abvb", [1, D], BF16)

            kb.dma("gpsimd", atmp[0:1, 0:1024], d_bv[:], writes=(B("atmp"),))
            kb.op("vector", lambda e: e.tensor_copy(c_bvb[:],
                                                    atmp[0:1, 0:1024]),
                  reads=(B("atmp"),), writes=(B("abvb"),))

            for b in range(BL):
                qkvstack = ExitStack()
                wqb = mksbuf(qkvstack, f"awq{b}", [128, 8192], BF16)
                wkb = mksbuf(qkvstack, f"awk{b}", [128, 8192], BF16)
                wvb = mksbuf(qkvstack, f"awv{b}", [128, 8192], BF16)
                for nm in (f"awq{b}", f"awk{b}", f"awv{b}"):
                    for ch in range(8):
                        bufs[f"{nm}c{ch}"] = Buf(f"{nm}c{ch}")
                stjobs = ([j for ch in range(8)
                           for j in ((f"awq{b}", d_wq, wqb, ch),
                                     (f"awk{b}", d_wk, wkb, ch))]
                          + [(f"awv{b}", d_wv, wvb, ch) for ch in range(8)])
                for st0_ in range(min(2, nst)):
                    kb.dma("sync",
                           x_in[st0_ % 2].ap().rearrange(
                               "p (c n) -> p c n", c=8),
                           d_x1t[b, :, st0_ * 256:
                                 (st0_ + 1) * 256].rearrange(
                               "(c p) n -> p c n", p=128),
                           writes=(B(f"ax{st0_ % 2}"),))
                for idx, (nm, ddr, wt, ch) in enumerate(stjobs):
                    pp = idx % 2
                    sg = astg[:, pp * 1024:(pp + 1) * 1024]
                    sgB = B(f"astg{pp}")
                    kb.dma("gpsimd" if pp == 0 else "scalar",
                           sg, ddr.ap().rearrange("(c p) n -> p c n",
                                                  p=128)[:, ch, :],
                           writes=(sgB,))
                    kb.op("vector", lambda e, w=wt, ch=ch, sg=sg:
                          e.tensor_copy(
                              w[:, ch * 1024:(ch + 1) * 1024], sg),
                          reads=(sgB,), writes=(B(f"{nm}c{ch}"),))

                def a_load(st):
                    par = st % 2
                    kb.dma("sync",
                           x_in[par].ap().rearrange("p (c n) -> p c n", c=8),
                           d_x1t[b, :, st * 256:(st + 1) * 256].rearrange(
                               "(c p) n -> p c n", p=128),
                           writes=(B(f"ax{par}"),))

                def a_ln_parts(st):
                    par = st % 2
                    lp = st % 2
                    return emit_ln_parts(
                        x_in[par], f"ax{par}",
                        lnx[:, lp * 2048:(lp + 1) * 2048], f"alnx{lp}",
                        xsq, "axsq", ab, "aab",
                        rows, "arw", atmp, "atmp")

                def a_ln(st):
                    a1m, a2p, a2m, bpre, bmms, bpost, c = a_ln_parts(st)
                    kb_mms(a1m()); a2p(); kb_mms(a2m()); bpre()
                    kb_mms(bmms()); bpost(); c()

                a_ln(0)
                for st in range(nst):
                    stg = a_ln_parts(st + 1) if st + 1 < nst else None
                    lnxv = lnx[:, (st % 2) * 2048:(st % 2) * 2048 + 2048]
                    lnxB_st = B(f"alnx{st % 2}")
                    for nm, wt, bias, bias_n, dstb, dst in (
                            (f"awq{b}", wqb, c_bq, "c_bq", "aq", q_buf),
                            (f"awk{b}", wkb, c_bk, "c_bk", "ak", k_buf)):
                        for grp in range(2):
                            for dk in range(8):
                                for hj in range(4):
                                    hc = grp * 4 + hj
                                    pj = grp * 4 + hj
                                    kb.mm(PB[pj][:, 0:256],
                                          wt[:, dk * 1024 + hc * 128:
                                             dk * 1024 + (hc + 1) * 128],
                                          lnxv[:, dk * 256:(dk + 1) * 256],
                                          256, B(f"pb{pj}"),
                                          start=(dk == 0), stop=(dk == 7),
                                          reads=(B(f"{nm}c{dk}"),
                                                 bufs.setdefault(
                                                     f"alnx{st % 2}d{dk}",
                                                     Buf(f"alnx{st % 2}"
                                                         f"d{dk}"))))
                            for hj in range(4):
                                hc = grp * 4 + hj
                                pj = grp * 4 + hj
                                kb.op("scalar", lambda e, hc=hc, pj=pj,
                                      w=dst, bs=bias, st=st: e.activation(
                                          w[:, hc * S + st * 256:
                                            hc * S + (st + 1) * 256],
                                          PB[pj][:, 0:256], AF.Identity,
                                          bias=bs[:, hc:hc + 1]),
                                      reads=(B(f"pb{pj}"), B(bias_n)),
                                      writes=(B(dstb),))
                    if stg is not None:
                        kb_mms(stg[0]())
                        stg[1]("vector")
                        kb_mms(stg[2]())
                        stg[3]()
                    for scl in range(2):
                        sc = st * 2 + scl
                        for dk in range(8):
                            if scl == 1 and dk == 2 and stg is not None:
                                kb_mms(stg[4]())
                            for hf in range(2):
                                kb.mm(PB[4 + hf][:, 0:512],
                                      lnxv[:, dk * 256 + scl * 128:
                                          dk * 256 + scl * 128 + 128],
                                      wvb[:, dk * 1024 + hf * 512:
                                          dk * 1024 + (hf + 1) * 512],
                                      512, B(f"pb{4 + hf}"),
                                      start=(dk == 0), stop=False,
                                      reads=(B(f"awv{b}c{dk}"),
                                             bufs.setdefault(
                                                 f"alnx{st % 2}d{dk}",
                                                 Buf(f"alnx{st % 2}"
                                                     f"d{dk}"))))
                        for hf in range(2):
                            kb.mm(PB[4 + hf][:, 0:512], c_onesr[:],
                                  c_bvb[:, hf * 512:(hf + 1) * 512], 512,
                                  B(f"pb{4 + hf}"), start=False, stop=True,
                                  reads=(B("abvb"), B("c_onesr")))
                        kb.op("scalar", lambda e, sc=sc: e.copy(
                            v_buf.ap().rearrange(
                                "p (a c) -> p a c", c=65)[
                                :, sc * 16:(sc + 1) * 16, 0:64],
                            PW[2][:, 0:1024].rearrange(
                                "p (a c) -> p a c", c=64)),
                            reads=(B("pb4"), B("pb5")),
                            writes=(B("av"),))
                    if stg is not None:
                        stg[5]()
                        stg[6]()
                    if st + 2 < nst:
                        a_load(st + 2)
                kb.op("vector", lambda e: e.memset(
                    v_buf.ap().rearrange("p (a c) -> p a c",
                                         c=65)[:, :, 64:65], 1.0),
                    reads=(), writes=(B("av"),))

                bufs["axbf0"] = Buf("axbf0")
                bufs["axbf1"] = Buf("axbf1")
                cjobs = [(j, srcb, src, hc, qq)
                         for j, (srcb, src) in enumerate(
                             (("aq", q_buf), ("ak", k_buf)))
                         for hc in range(8) for qq in range(S // 512)]

                def c_sq(i):
                    j, srcb, src, hc, qq = cjobs[i]
                    sl = slice(hc * S + qq * 512, hc * S + qq * 512 + 512)
                    pp = i % 2
                    xsl = xbf[:, pp * 512:pp * 512 + 512]
                    kb.op("vector", lambda e, src=src, sl=sl, xsl=xsl:
                          e.tensor_tensor(xsl, src[:, sl],
                                          src[:, sl], OP.mult),
                          reads=(B(srcb),), writes=(B(f"axbf{pp}"),))

                c_sq(0)
                for i in range(len(cjobs)):
                    j, srcb, src, hc, qq = cjobs[i]
                    pp = i % 2
                    xsl = xbf[:, pp * 512:pp * 512 + 512]
                    pbk = 6 + pp
                    kb.mm(PB[pbk][0:1, 0:512], c_ones[:, 0:1],
                          xsl, 512, B(f"pb{pbk}"),
                          start=True, stop=True,
                          reads=(B(f"axbf{pp}"), B("c_ones")))
                    if i + 1 < len(cjobs):
                        c_sq(i + 1)
                    kb.op("vector", lambda e, pbk=pbk, pp=pp:
                          e.tensor_reduce(
                        s_cden[0:1, 4 + pp:5 + pp], PB[pbk][0:1, 0:512],
                        AX, OP.max),
                        reads=(B(f"pb{pbk}"),), writes=(B("s_cden"),))
                    if hc == 0 and qq == 0:
                        kb.op("vector", lambda e, j=j, pp=pp:
                              e.tensor_copy(s_cden[0:1, j:j + 1],
                                            s_cden[0:1, 4 + pp:5 + pp]),
                              reads=(B("s_cden"),),
                              writes=(B("s_cden"),))
                    else:
                        kb.op("vector", lambda e, j=j, pp=pp:
                              e.tensor_tensor(
                                  s_cden[0:1, j:j + 1],
                                  s_cden[0:1, j:j + 1],
                                  s_cden[0:1, 4 + pp:5 + pp], OP.max),
                              reads=(B("s_cden"),),
                              writes=(B("s_cden"),))
                kb.op("vector", lambda e: e.tensor_tensor(
                    s_cden[0:1, 2:3], s_cden[0:1, 0:1], s_cden[0:1, 1:2],
                    OP.mult), reads=(B("s_cden"),), writes=(B("s_cden"),))
                kb.op("scalar", lambda e: e.activation(
                    s_cden[0:1, 2:3], s_cden[0:1, 2:3], AF.Ln),
                    reads=(B("s_cden"),), writes=(B("s_cden"),))
                kb.op("scalar", lambda e: e.activation(
                    s_cden[0:1, 2:3], s_cden[0:1, 2:3], AF.Exp,
                    scale=c_f32[0:1, 5:6]),
                    reads=(B("s_cden"), B("c_f32")), writes=(B("s_cden"),))
                kb.op("vector", lambda e: e.tensor_scalar(
                    s_cden[0:1, 3:4], s_cden[0:1, 2:3], -0.125, None,
                    OP.mult), reads=(B("s_cden"),), writes=(B("s_cden"),))
                kb.op("vector", lambda e: e.tensor_copy(
                    s_cbf[0:1, 0:1], s_cden[0:1, 3:4]),
                    reads=(B("s_cden"),), writes=(B("s_cbf"),))
                kb.mm(PB[7][0:128, 0:1], c_onesr[:],
                      s_cbf[0:1, 0:1], 1, B("pb7"),
                      start=True, stop=True,
                      reads=(B("s_cbf"), B("c_onesr")))
                kb.op("scalar", lambda e, b=b: e.copy(
                    s_negc[:, b:b + 1], PB[7][0:128, 0:1]),
                    reads=(B("pb7"),), writes=(B("s_negc"),))

                kb.op("vector", lambda e: e.tensor_copy(
                    xbf[:, 0:1], xbf[:, 0:1]),
                    reads=(B("axbf0"), B("axbf1")), writes=(B("axbf"),))
                del bufs["axbf0"]
                del bufs["axbf1"]
                kb.barrier()
                qkvstack.close()
                for n in (f"awq{b}", f"awk{b}", f"awv{b}"):
                    del bufs[n]
                    for ch in range(8):
                        del bufs[f"{n}c{ch}"]

                corestack = ExitStack()
                wob = mksbuf(corestack, f"awo{b}", [128, 8192], BF16)
                expb = [mksbuf(corestack, f"ae{i}_{b}", [128, 1024], BF16)
                        for i in range(2)]
                wv_bf = mksbuf(corestack, f"awvo{b}", [128, 8 * 512], BF16)
                den = mksbuf(corestack, f"aden{b}", [1, 1024], F32)
                recip = mksbuf(corestack, f"arc{b}", [1, 1024], F32)
                o_sb = mksbuf(corestack, f"ao{b}", [128, 2 * 512], F32)
                bufs[f"ao0_{b}"] = Buf(f"ao0_{b}")
                bufs[f"ao1_{b}"] = Buf(f"ao1_{b}")
                del bufs[f"ao{b}"]
                rb_sb = mksbuf(corestack, f"arb{b}", [128, 512], F32)
                wvu = mksbuf(corestack, f"awvu{b}", [128, 512], BF16)
                for ch in range(4):
                    kb.dma("gpsimd" if ch % 2 == 0 else "scalar",
                           atmp.ap().rearrange("p (c n) -> p c n", c=2),
                           d_wo.ap().rearrange("(c p) n -> p c n",
                                               p=128)[:, 2 * ch:2 * ch + 2,
                                                      :],
                           writes=(B("atmp"),))
                    kb.op("vector", lambda e, ch=ch, wob=wob:
                          e.tensor_copy(
                        wob[:, ch * 2048:(ch + 1) * 2048], atmp[:]),
                        reads=(B("atmp"),), writes=(B(f"awo{b}"),))

                pending_tail = [None]
                pending_tail_pe = [None]
                for qs in range(nqs):
                    nsc = 4 * qs + 4
                    for p in range(8):
                        def scores(sc):
                            par2 = sc % 2
                            m_d = sc - 4 * qs
                            u0 = 128 * m_d if m_d >= 0 else 0
                            t1, t2 = PB[2 * par2], PB[2 * par2 + 1]
                            b1 = B(f"pb{2 * par2}")
                            b2 = B(f"pb{2 * par2 + 1}")
                            eb = B(f"ae{par2}_{b}")
                            ept = expb[par2]
                            kb.mm(t1[:, u0:512],
                                  k_buf[0:64, p * S + sc * 128:
                                        p * S + (sc + 1) * 128],
                                  q_buf[0:64, p * S + qs * 512 + u0:
                                        p * S + (qs + 1) * 512],
                                  512 - u0, b1, start=True, stop=True,
                                  reads=(B("ak"), B("aq")),
                                  tile_position=(0, 0))
                            kb.mm(t2[:, u0:512],
                                  k_buf[64:128, p * S + sc * 128:
                                        p * S + (sc + 1) * 128],
                                  q_buf[64:128, p * S + qs * 512 + u0:
                                        p * S + (qs + 1) * 512],
                                  512 - u0, b2, start=True, stop=True,
                                  reads=(B("ak"), B("aq")),
                                  tile_position=(64, 0))
                            if u0 == 0:
                                kb.op("scalar", lambda e, par2=par2,
                                      ep=ept, b=b: e.activation(
                                          ep[:, 0:1024],
                                          PW[par2][:, 0:1024], AF.Exp,
                                          bias=s_negc[:, b:b + 1],
                                          scale=c_f32[:, 0:1]),
                                      reads=(b1, b2, B("s_negc"),
                                             B("c_f32")),
                                      writes=(eb,))
                            else:
                                kb.op("scalar", lambda e, u0=u0, par2=par2,
                                      ep=ept, b=b: e.activation(
                                          ep.rearrange(
                                              "p (c n) -> p c n",
                                              c=2)[:, :, u0:512],
                                          PW[par2].rearrange(
                                              "p (c n) -> p c n",
                                              c=2)[:, :, u0:512], AF.Exp,
                                          bias=s_negc[:, b:b + 1],
                                          scale=c_f32[:, 0:1]),
                                      reads=(b1, b2, B("s_negc"),
                                             B("c_f32")),
                                      writes=(eb,))
                            if m_d >= 0:
                                for off in (u0, 512 + u0):
                                    kb.op("vector", lambda e, off=off,
                                          ep=ept: e.tensor_tensor(
                                              ep[:, off:off + 128],
                                              ep[:, off:off + 128],
                                              c_mask[:], OP.mult),
                                          reads=(eb, B("c_mask")),
                                          writes=(eb,))

                        nvleft = [8]

                        def av_reg(sc):
                            par2 = sc % 2
                            m_d = sc - 4 * qs
                            u0 = 128 * m_d if m_d >= 0 else 0
                            lo = u0 + 128 if m_d >= 0 else 0
                            if lo >= 512:
                                return
                            eb = B(f"ae{par2}_{b}")
                            for hh in range(2):
                                kb.mm(PB[4 + hh][0:65, lo:512],
                                      v_buf[:, sc * 1040
                                            + (2 * p + hh) * 65:
                                            sc * 1040 + (2 * p + hh) * 65
                                            + 65],
                                      expb[par2][:, hh * 512 + lo:
                                                 hh * 512 + 512],
                                      512 - lo, B(f"pb{4 + hh}"),
                                      start=(sc == 0), stop=False,
                                      reads=(B("av"), eb))

                        def av_msk(sc):
                            if sc < 0:
                                return
                            m_d = sc - 4 * qs
                            if m_d < 0:
                                return
                            par2 = sc % 2
                            u0 = 128 * m_d
                            eb = B(f"ae{par2}_{b}")
                            for hh in range(2):
                                nvleft[0] -= 1
                                kb.mm(PB[4 + hh][0:65, u0:u0 + 128],
                                      v_buf[:, sc * 1040
                                            + (2 * p + hh) * 65:
                                            sc * 1040 + (2 * p + hh) * 65
                                            + 65],
                                      expb[par2][:, hh * 512 + u0:
                                                 hh * 512 + u0 + 128],
                                      128, B(f"pb{4 + hh}"),
                                      start=False, stop=(nvleft[0] < 2),
                                      reads=(B("av"), eb))

                        scores(0)
                        if pending_tail[0] is not None:
                            pending_tail[0]()
                            pending_tail[0] = None
                        for sc in range(1, nsc):
                            av_msk(sc - 2)
                            scores(sc)
                            av_reg(sc - 1)
                        av_reg(nsc - 1)
                        av_msk(nsc - 2)
                        av_msk(nsc - 1)
                        if pending_tail_pe[0] is not None:
                            pending_tail_pe[0]()
                            pending_tail_pe[0] = None

                        def tail_a(p=p):
                            kb.op("vector", lambda e, den=den:
                                  e.tensor_copy(
                                den[0:1, 0:512], PB[4][64:65, 0:512]),
                                reads=(B("pb4"),), writes=(B(f"aden{b}"),))
                            kb.op("vector", lambda e, den=den:
                                  e.tensor_copy(
                                den[0:1, 512:1024], PB[5][64:65, 0:512]),
                                reads=(B("pb5"),), writes=(B(f"aden{b}"),))
                            for hh in range(2):
                                kb.op("vector", lambda e, hh=hh, wvu=wvu:
                                      e.tensor_copy(
                                    wvu[hh * 64:(hh + 1) * 64, 0:512],
                                    PB[4 + hh][0:64, 0:512]),
                                    reads=(B(f"pb{4 + hh}"),),
                                    writes=(B(f"awvu{b}"),))
                            kb.op("scalar", lambda e, den=den:
                                  e.activation(den[:], den[:], AF.Ln),
                                reads=(B(f"aden{b}"),),
                                writes=(B(f"aden{b}"),))
                            kb.op("scalar", lambda e, den=den,
                                  recip=recip: e.activation(
                                recip[:], den[:], AF.Exp,
                                scale=c_f32[0:1, 3:4]),
                                reads=(B(f"aden{b}"), B("c_f32")),
                                writes=(B(f"arc{b}"),))

                        def tail_pe(p=p):
                            for hh in range(2):
                                kb.mm(PB[6][hh * 64:(hh + 1) * 64, 0:512],
                                      c_onesf[0:1, 0:64],
                                      recip[0:1, hh * 512:(hh + 1) * 512],
                                      512, B("pb6"),
                                      start=True, stop=True,
                                      reads=(B(f"arc{b}"), B("c_onesf")))
                            kb.op("vector", lambda e, rb_sb=rb_sb:
                                  e.tensor_copy(
                                rb_sb[:], PB[6][0:128, 0:512]),
                                reads=(B("pb6"),), writes=(B(f"arb{b}"),))
                            kb.op("vector", lambda e, p=p, wvu=wvu,
                                  wv_bf=wv_bf, rb_sb=rb_sb:
                                  e.tensor_tensor(
                                      wv_bf[:, p * 512:(p + 1) * 512],
                                      wvu[:, 0:512],
                                      rb_sb[:, 0:512],
                                      OP.mult),
                                  reads=(B(f"awvu{b}"), B(f"arb{b}")),
                                  writes=(B(f"awvo{b}"),))

                        pending_tail[0] = tail_a
                        pending_tail_pe[0] = tail_pe
                    pending_tail[0]()
                    pending_tail[0] = None
                    pending_tail_pe[0]()
                    pending_tail_pe[0] = None
                    for half in range(2):
                        kb.dma("sync",
                               x_in[half].ap().rearrange(
                                   "p (c n) -> p c n", c=8),
                               d_x1t[b, :, qs * 512 + half * 256:
                                     qs * 512 + (half + 1) * 256].rearrange(
                                   "(c p) n -> p c n", p=128),
                               writes=(B(f"ax{half}"),))
                    for dkg in range(2):
                        for hc in range(8):
                            for dkj in range(4):
                                dk = dkg * 4 + dkj
                                pj = dkg * 4 + dkj
                                kb.mm(PB[pj][:, 0:512],
                                      wob[:, hc * 1024 + dk * 128:
                                          hc * 1024 + (dk + 1) * 128],
                                      wv_bf[:, hc * 512:(hc + 1) * 512],
                                      512, B(f"pb{pj}"),
                                      start=(hc == 0), stop=(hc == 7),
                                      reads=(B(f"awo{b}"), B(f"awvo{b}")))
                        for dkj in range(4):
                            dk = dkg * 4 + dkj
                            pj = dkg * 4 + dkj
                            pt, pbn = PB[pj], B(f"pb{pj}")
                            par_o = dk % 2
                            osl = o_sb[:, par_o * 512:(par_o + 1) * 512]
                            aob = B(f"ao{par_o}_{b}")
                            kb.op("scalar", lambda e, pt=pt, dk=dk,
                                  osl=osl:
                                  e.activation(osl, pt[:, 0:512],
                                               AF.Identity,
                                               bias=c_bo[:, dk:dk + 1]),
                                  reads=(pbn, B("c_bo")),
                                  writes=(aob,))
                            for half in range(2):
                                kb.op("vector", lambda e, dk=dk, half=half,
                                      osl=osl:
                                      e.tensor_tensor(
                                          x_in[half][:, dk * 256:
                                                     (dk + 1) * 256],
                                          x_in[half][:, dk * 256:
                                                     (dk + 1) * 256],
                                          osl[:, half * 256:
                                              (half + 1) * 256],
                                          OP.add),
                                      reads=(B(f"ax{half}"), aob),
                                      writes=(B(f"ax{half}"),))
                    for half in range(2):
                        kb.dma("scalar",
                               d_x2t[b, :, qs * 512 + half * 256:
                                     qs * 512 + (half + 1) * 256].rearrange(
                                   "(c p) n -> p c n", p=128),
                               x_in[half].ap().rearrange(
                                   "p (c n) -> p c n", c=8),
                               reads=(B(f"ax{half}"),))
                if debug_outputs and b == 0:
                    kb.dma("sync", d_dbgq[:], q_buf[:, 0:8 * S],
                           reads=(B("aq"),))
                    kb.dma("sync", d_dbgk[:], k_buf[:], reads=(B("ak"),))
                    kb.dma("sync", d_dbgv[:], v_buf[:], reads=(B("av"),))
                    kb.dma("sync", d_dbgw[:], wv_bf[:],
                           reads=(B(f"awvo{b}"),))
                    kb.dma("sync", d_dbgc[:], s_negc[:],
                           reads=(B("s_negc"),))
                kb.barrier()
                corestack.close()
                for n in (f"awo{b}", f"ae0_{b}", f"ae1_{b}", f"awvo{b}",
                          f"aden{b}", f"arc{b}", f"ao0_{b}", f"ao1_{b}",
                          f"arb{b}", f"awvu{b}"):
                    del bufs[n]
            ph.close()
            for n in ("aq", "ak", "av", "ax0", "ax1", "alnx0", "alnx1",
                      "axbf", "axsq", "aab", "arw", "atmp", "abvb", "astg",
                      "astg0", "astg1"):
                del bufs[n]
            for n in [k2 for k2 in bufs if k2.startswith("alnx")]:
                del bufs[n]

        emit_ffn(d_xt, d_x1t, first=True, tag="f1")
        emit_attn()
        emit_ffn(d_x2t, d_out, first=False, tag="f3")

        with ExitStack() as semstack:
            for s in kb.all_sems:
                s.h = semstack.enter_context(nc.semaphore(s.name))
            block = top.enter_context(nc.Block())
            kb.replay(block)

    return nc, kb



def _layouts(L0_attn_inputs, L0_attn_W_q, L0_attn_b_q, L0_attn_W_k,
             L0_attn_b_k, L0_attn_W_v, L0_attn_b_v, L0_attn_W_o, L0_attn_b_o,
             L0_ffn_W_fc, L0_ffn_W_proj):
    f32 = np.float32
    xT = np.ascontiguousarray(
        np.asarray(L0_attn_inputs, f32).transpose(0, 2, 1))
    lay = dict(
        wfc=np.ascontiguousarray(np.asarray(L0_ffn_W_fc, f32).T),
        wpr=np.ascontiguousarray(np.asarray(L0_ffn_W_proj, f32).T),
        wq=np.ascontiguousarray(
            np.asarray(L0_attn_W_q, f32).transpose(1, 0, 2).reshape(D, D)),
        wk=np.ascontiguousarray(
            np.asarray(L0_attn_W_k, f32).transpose(1, 0, 2).reshape(D, D)),
        wv=np.ascontiguousarray(
            np.asarray(L0_attn_W_v, f32).transpose(1, 0, 2).reshape(D, D)),
        wo=np.ascontiguousarray(np.asarray(L0_attn_W_o, f32).T),
        bq=np.ascontiguousarray(np.asarray(L0_attn_b_q, f32).reshape(8, 128).T),
        bk=np.ascontiguousarray(np.asarray(L0_attn_b_k, f32).reshape(8, 128).T),
        bo=np.ascontiguousarray(np.asarray(L0_attn_b_o, f32).reshape(8, 128).T),
        bv=np.ascontiguousarray(np.asarray(L0_attn_b_v, f32).reshape(1, D)),
        mask=np.ascontiguousarray(
            np.triu(np.ones((128, 128))).astype(ml_dtypes.bfloat16)),
        ones=np.ones((128, 1), dtype=ml_dtypes.bfloat16),
        onesr=np.ones((1, 128), dtype=ml_dtypes.bfloat16),
        cf32=np.ascontiguousarray(
            np.tile(np.array([[0.125, 2.0, -1.2, -1.0, -0.5, 0.5]], f32),
                    (128, 1))),
        onesf=np.ones((1, 128), dtype=f32),
        onesc=np.ones((128, 1), dtype=f32),
    )
    return xT, lay


def kernel(**inputs):
    xT, lay = _layouts(**inputs)
    nc, _kb = build_nc(S_FULL)
    in_maps = []
    for c in range(NCORE):
        m = {"xt": np.ascontiguousarray(xT[c * BL:(c + 1) * BL])}
        m.update(lay)
        in_maps.append(m)
    res = run_bass_kernel_spmd(nc, in_maps, core_ids=list(range(NCORE)))
    outs = [res.results[c]["out"] for c in range(NCORE)]
    full = np.concatenate(outs, axis=0)
    return np.ascontiguousarray(full.transpose(0, 2, 1)).astype(np.float32)
